# version 08
# baseline (speedup 1.0000x reference)
import math
import numpy as np
from contextlib import ExitStack

import concourse.bass as bass
import concourse.mybir as mybir
from concourse.bass_utils import run_bass_kernel_spmd

F32 = mybir.dt.float32
BF16 = mybir.dt.bfloat16
I32 = mybir.dt.int32
AF = mybir.ActivationFunctionType
ALU = mybir.AluOpType
AX = mybir.AxisListType

NCORES = 8
B, T, D = 16, 2048, 1024
NSEQ = B // NCORES
H, DH = 16, 64
DC = D // 128
DFF = 4096
NE, M = 16, 512
QC = 512
NQC = T // QC
NKT = T // 128
EPS = 1e-5
LN1E4 = math.log(10000.0)
TWO_PI = 2.0 * math.pi
SHUF = [i + 1 - 2 * (i % 2) for i in range(32)]

AR_N = 84224
R_O = 0
HH_O = 16384
XT_O = 32768
HT_O = 36864
VS_O = 32768
TA_O = 34816
TB_O = 43008
KR_O = 32768
VT_O = 49152
QQ_O = 65792
OTH_O = 69888
EA_O = 73984
EB_O = 75008
WQ_O = 80128
WO_O = 82176
KW_O = 76032
GG_O = 32768
H1F_O = 49152
W1_O = 53248
W2_O = 61440
ZZ_O = 16384
HM_O = 51200
WG_O = 55296
WU_O = 63488
WDM_O = 71680
E16_O = 79872
OF_O = 16384
OU_O = 24576


class Emitter:
    def __init__(self):
        self.tok = {}
        self.semnames = (["syncq", "poolq", "actq", "pe", "dve", "act", "pool",
                          "vsq", "dbgq", "xq0", "xq1", "st0", "st1"] +
                         [f"ws_{s}{h}" for s in ("kw", "wq", "wo", "w1", "w2",
                                                 "wg", "wu", "wd")
                          for h in (0, 1)] +
                         [f"pq_{n}" for n in ("Wqkv", "Wo", "W1", "W2",
                                              "ws_gate", "ws_up", "ws_down")] +
                         [f"pq_e{e}" for e in range(16)])

    def begin(self, cur, engines, sems):
        self.cur = cur
        self.eng = engines
        self.sems = sems
        self.cnt = {k: 0 for k in self.semnames}
        self.lastwait = {}
        self.dry = cur is None

    def _resolve(self, w):
        if isinstance(w, tuple):
            return w
        assert w in self.tok, f"wait on undefined token {w}"
        return self.tok[w]

    def op(self, engine, fn, waits=(), inc=None, n=1, tok=None, isdma=False):
        chain = (engine in ("dve", "act", "pool")) and not isdma
        esem = {"sync": "syncq", "pool": "pool", "pe": "pe",
                "dve": "dve", "act": "act"}[engine]
        if inc is None and (tok is not None or chain):
            inc = esem
        allwaits = list(waits)
        if chain and self.cnt[esem] > 0:
            allwaits.append((esem, self.cnt[esem]))
        if self.cur == engine:
            h = self.eng
            byname = {}
            for w in allwaits:
                sem, val = self._resolve(w)
                byname[sem] = max(byname.get(sem, -1), val)
            for sem, val in byname.items():
                if self.lastwait.get(sem, -1) < val:
                    h.wait_ge(self.sems[sem], val)
                    self.lastwait[sem] = val
            instr = fn(h)
            if inc is not None:
                instr.then_inc(self.sems[inc], n)
        elif not self.dry:
            for w in allwaits:
                self._resolve(w)
        if inc is not None:
            self.cnt[inc] += n
            if tok is not None:
                self.tok[tok] = (inc, self.cnt[inc])

    def waits_only(self, engine, waits):
        if self.cur == engine:
            h = self.eng
            byname = {}
            for w in waits:
                sem, val = self._resolve(w)
                byname[sem] = max(byname.get(sem, -1), val)
            for sem, val in byname.items():
                if self.lastwait.get(sem, -1) < val:
                    h.wait_ge(self.sems[sem], val)
                    self.lastwait[sem] = val
        elif not self.dry:
            for w in waits:
                self._resolve(w)

    def dma(self, queue, out, in_, waits=(), tok=None, sem=None):
        semname = sem or {"sync": "syncq", "pool": "poolq", "act": "actq"}[queue]
        self.op(queue, lambda e: e.dma_start(out=out, in_=in_),
                waits, inc=semname, n=16, tok=tok, isdma=True)

    def mm(self, out, lhsT, rhs, start=True, stop=True, waits=(), tok=None):
        self.op("pe", lambda e: e.matmul(out, lhsT, rhs, start=start, stop=stop),
                waits, inc="pe" if tok else None, n=1, tok=tok)

    def tr(self, out, in_, ident, waits=(), tok=None):
        self.op("pe", lambda e: e.transpose(out, in_, ident),
                waits, inc="pe" if tok else None, n=1, tok=tok)


def build_nc(dbg=None, nseq=NSEQ, phases=('load','kv','attn','ffn','moe')):
    dbg = dbg or set()
    nc = bass.Bass("TRN2", target_bir_lowering=False)
    em = Emitter()

    x_h = nc.dram_tensor("x", [nseq * T, D], F32, kind="ExternalInput").ap()
    out_h = nc.dram_tensor("out", [nseq * T, D], F32, kind="ExternalOutput").ap()
    wf = {}
    for name, shape in [
        ("Wqkv", [D, 3 * D]), ("Wo", [D, D]), ("W1", [D, DFF]), ("W2", [DFF, D]),
        ("ln1_g", [D]), ("ln1_b", [D]), ("ln2_g", [D]), ("ln2_b", [D]),
        ("b1", [DFF]), ("b2", [D]), ("gate_w", [NE, D]),
        ("w_gate", [NE, D, M]), ("w_up", [NE, D, M]), ("w_down", [NE, M, D]),
        ("ws_gate", [D, M]), ("ws_up", [D, M]), ("ws_down", [M, D]),
    ]:
        wf[name] = nc.dram_tensor(name, shape, F32, kind="ExternalInput").ap()

    scr = {}
    scr_meta = {}

    def mk_scr(name, K, Mtot, mw):
        KC = K // 128
        nmg = Mtot // mw
        scr[name] = nc.dram_tensor("scr_" + name, [nmg, 128, KC * mw], BF16,
                                   kind="Internal").ap()
        scr_meta[name] = (KC, Mtot, mw, nmg)

    mk_scr("Wqkv", D, 3 * D, 256)
    mk_scr("Wo", D, D, 128)
    mk_scr("W1", D, DFF, 512)
    mk_scr("W2", DFF, D, 128)
    mk_scr("ws_gate", D, M, 512)
    mk_scr("ws_up", D, M, 512)
    mk_scr("ws_down", M, D, 1024)
    for e in range(NE):
        mk_scr(f"wg{e}", D, M, 512)
        mk_scr(f"wu{e}", D, M, 512)
        mk_scr(f"wd{e}", M, D, 1024)

    dbg_outs = {}
    def mkdbg(name, shape, dtype=F32):
        if name in dbg:
            dbg_outs[name] = nc.dram_tensor("dbg_" + name, shape, dtype,
                                            kind="ExternalOutput").ap()
    for s in range(nseq):
        for nm in ("h0", "R0", "kr", "x1", "xa"):
            mkdbg(f"s{s}_{nm}", [128, DC * T])
        mkdbg(f"s{s}_vt", [128, NKT * H * 65])
        mkdbg(f"s{s}_q0", [128, DC * QC])
        mkdbg(f"s{s}_ot0", [128, DC * QC])
        mkdbg(f"s{s}_hm0", [128, DC * QC])
        mkdbg(f"s{s}_scores0", [16, QC])
        mkdbg(f"s{s}_routeT0", [16, QC])
        mkdbg(f"s{s}_cneg", [128, 16])
        mkdbg(f"s{s}_z00", [128, 4 * QC])
        mkdbg(f"s{s}_g00", [128, 32 * QC])

    with ExitStack() as ctx:
        def sb(name, shape, dtype):
            return ctx.enter_context(nc.sbuf_tensor(name, shape, dtype))

        ARENA = sb("ARENA", [128, AR_N], BF16)
        identf = sb("identf", [128, 128], F32)
        ones1b = sb("ones1b", [1, 128], BF16)
        onescb = sb("onescb", [128, 1], BF16)
        COS = sb("COS", [128, T], BF16)
        SINPM = sb("SINPM", [128, T], BF16)
        maskB = sb("maskB", [128, 896], BF16)
        ln1g = sb("ln1g", [128, DC], F32)
        ln1b_ = sb("ln1b", [128, DC], F32)
        ln2g = sb("ln2g", [128, DC], F32)
        ln2b_ = sb("ln2b", [128, DC], F32)
        b2c = sb("b2c", [128, DC], F32)
        b1c = sb("b1c", [128, DFF // 128], F32)
        gwT = sb("gwT", [128, DC * 16], BF16)
        consts = sb("consts", [128, 8], F32)
        cnegc = sb("cnegc", [128, 16], F32)
        pcol = sb("pcol", [128, 4], F32)
        rowb = sb("rowb", [1, 5248], BF16)
        rowf = sb("rowf", [1, 1664], F32)
        bc1 = sb("bc1", [128, QC], BF16)
        bc2 = sb("bc2", [128, QC], BF16)
        sgf = sb("sgf", [128, QC], F32)
        z1f = sb("z1f", [128, QC], F32)
        sqb = sb("sqb", [128, QC], BF16)
        stat = sb("stat", [128, 16], F32)
        lrow = sb("lrow", [1, 1024], BF16)
        cnegP = sb("cnegP", [128, 8], F32)
        rpA = sb("rpA", [128, QC], BF16)
        rpB = sb("rpB", [128, QC], BF16)
        knb = sb("knb", [128, QC], BF16)

        PS4 = ctx.enter_context(nc.psum_tensor("PS4", [128, 2048], F32))
        PA = PS4[:, 0:1024]
        PB = PS4[:, 1024:2048]
        PC = ctx.enter_context(nc.psum_tensor("PC", [128, 1024], F32))
        PD = ctx.enter_context(nc.psum_tensor("PD", [128, 512], F32))
        PEb = ctx.enter_context(nc.psum_tensor("PEb", [128, 512], F32))

        sems = {}
        for s_ in em.semnames:
            sems[s_] = ctx.enter_context(nc.semaphore(s_))

        S = dict(locals())

        def sched():
            build_schedule(em, S, nseq, dbg_outs, phases)

        block = ctx.enter_context(nc.Block())
        em.begin(None, None, sems)
        sched()
        for key, attr in (("sync", "sync"), ("pool", "gpsimd"), ("pe", "tensor"),
                          ("dve", "vector"), ("act", "scalar")):
            def body(e, key=key):
                em.begin(key, e, sems)
                sched()
            getattr(block, attr)(body)

    nc._dbg_outs = dbg_outs
    nc._out_ap = out_h
    return nc


def build_schedule(em, S, nseq, dbg_outs, phases):
    ARENA = S["ARENA"]
    x_h, out_h, wf, scr, scr_meta = (S["x_h"], S["out_h"], S["wf"], S["scr"],
                                     S["scr_meta"])
    identf, ones1b, onescb = S["identf"], S["ones1b"], S["onescb"]
    COS, SINPM, maskB = S["COS"], S["SINPM"], S["maskB"]
    ln1g, ln1b_, ln2g, ln2b_ = S["ln1g"], S["ln1b_"], S["ln2g"], S["ln2b_"]
    b2c, b1c, gwT, consts = S["b2c"], S["b1c"], S["gwT"], S["consts"]
    cnegc, pcol = S["cnegc"], S["pcol"]
    rowb, rowf = S["rowb"], S["rowf"]
    bc1, bc2, sgf, z1f, sqb = S["bc1"], S["bc2"], S["sgf"], S["z1f"], S["sqb"]
    stat, lrow = S["stat"], S["lrow"]
    PA, PB, PC, PD, PEb = S["PA"], S["PB"], S["PC"], S["PD"], S["PEb"]
    PS4, cnegP = S["PS4"], S["cnegP"]
    rpA, rpB, knb = S["rpA"], S["rpB"], S["knb"]

    def A(off, n):
        return ARENA[:, off:off + n]

    def Af(off, n):
        return ARENA[:, off:off + 2 * n].bitcast(F32)

    SCT = Af(81920, 512)[0:16, :]
    RTT = A(82944, 512)[0:16, :]
    SCt = Af(83456, 64)

    dbg_on = lambda nm: nm in dbg_outs

    def dbg_store(nm, src_ap, waits):
        if dbg_on(nm):
            em.dma("pool", dbg_outs[nm], src_ap, waits=waits, sem="dbgq")

    def dbg_fence():
        if dbg_outs:
            em.op("dve", lambda e: e.memset(stat[:, 7:8], 0.0),
                  waits=[("dbgq", em.cnt["dbgq"])])
            em.op("act", lambda e: e.activation(stat[:, 7:8], stat[:, 7:8],
                                                AF.Copy),
                  waits=[("dbgq", em.cnt["dbgq"]), ("dve", em.cnt["dve"])])

    em.op("pool", lambda e: e.memset(consts[:, 1:8], 0.0))
    em.op("pool", lambda e: e.memset(consts[:, 0:1], EPS))
    em.op("pool", lambda e: e.memset(ones1b[:, :], 1.0))
    em.op("pool", lambda e: e.memset(onescb[:, :], 1.0))
    em.op("pool", lambda e: e.memset(Af(TA_O, 128)[:, 0:128], 1.0),
          tok="po_ones")
    em.op("pool", lambda e: e.affine_select(identf[:, :], Af(TA_O, 128)[:, 0:128],
                                            [[1, 128]], ALU.is_equal, 0.0,
                                            base=0, channel_multiplier=-1),
          waits=["po_ones"], tok="po_ident")
    em.op("pool", lambda e: e.memset(maskB[:, :], 1.0), tok="po_mb")
    em.op("pool", lambda e: e.affine_select(maskB[:, :], maskB[:, :],
                                            [[1, 896]], ALU.is_ge, 0.0,
                                            base=-384, channel_multiplier=-1),
          waits=["po_mb"], tok="po_mask")
    em.op("pool", lambda e: e.iota(Af(TA_O, T)[:, :], [[1, T]], base=0,
                                   channel_multiplier=0,
                                   allow_small_or_imprecise_dtypes=True),
          waits=["po_ident"], tok="po_iota")
    em.op("pool", lambda e: e.iota(pcol[:, 0:1], [[0, 1]], base=0,
                                   channel_multiplier=1,
                                   allow_small_or_imprecise_dtypes=True),
          tok="iota_done")
    em.op("dve", lambda e: e.tensor_scalar(pcol[:, 1:2], pcol[:, 0:1], 1.0 / 64,
                                           None, ALU.mult),
          waits=["iota_done", "po_iota"])
    em.op("dve", lambda e: e.tensor_copy(pcol[:, 2:3].bitcast(I32), pcol[:, 1:2]))
    em.op("dve", lambda e: e.tensor_copy(pcol[:, 1:2], pcol[:, 2:3].bitcast(I32)))
    em.op("dve", lambda e: e.tensor_scalar(pcol[:, 2:3], pcol[:, 1:2], 64.0, None,
                                           ALU.mult))
    em.op("dve", lambda e: e.tensor_scalar(pcol[:, 2:3], pcol[:, 2:3],
                                           pcol[:, 0:1], None, ALU.is_gt))
    em.op("dve", lambda e: e.tensor_sub(pcol[:, 1:2], pcol[:, 1:2], pcol[:, 2:3]))
    em.op("dve", lambda e: e.scalar_tensor_tensor(pcol[:, 1:2], pcol[:, 1:2],
                                                  -64.0, pcol[:, 0:1],
                                                  ALU.mult, ALU.add))
    em.op("dve", lambda e: e.tensor_scalar(pcol[:, 2:3], pcol[:, 1:2], 0.5, None,
                                           ALU.mult))
    em.op("dve", lambda e: e.tensor_copy(pcol[:, 3:4].bitcast(I32), pcol[:, 2:3]))
    em.op("dve", lambda e: e.tensor_copy(pcol[:, 2:3], pcol[:, 3:4].bitcast(I32)))
    em.op("dve", lambda e: e.tensor_scalar(pcol[:, 3:4], pcol[:, 2:3], 2.0, None,
                                           ALU.mult))
    em.op("dve", lambda e: e.tensor_scalar(pcol[:, 3:4], pcol[:, 3:4],
                                           pcol[:, 1:2], None, ALU.is_gt))
    em.op("dve", lambda e: e.tensor_sub(pcol[:, 2:3], pcol[:, 2:3], pcol[:, 3:4]))
    em.op("dve", lambda e: e.scalar_tensor_tensor(pcol[:, 3:4], pcol[:, 2:3], -2.0,
                                                  pcol[:, 1:2], ALU.mult, ALU.add))
    em.op("dve", lambda e: e.tensor_scalar(pcol[:, 3:4], pcol[:, 3:4], 2.0, -1.0,
                                           ALU.mult, ALU.add), tok="jcol")
    em.op("act", lambda e: e.activation(pcol[:, 1:2], pcol[:, 2:3], AF.Exp,
                                        scale=-LN1E4 / 32), waits=["jcol"],
          tok="invf")
    em.op("dve", lambda e: e.tensor_scalar(Af(TA_O, T)[:, :], Af(TA_O, T)[:, :],
                                           pcol[:, 1:2], None, ALU.mult),
          waits=["invf"])
    em.op("dve", lambda e: e.tensor_scalar(Af(TB_O, T)[:, :], Af(TA_O, T)[:, :],
                                           1.0 / TWO_PI, None, ALU.mult))
    em.op("dve", lambda e: e.tensor_copy(Af(TB_O, T)[:, :].bitcast(I32),
                                         Af(TB_O, T)[:, :]))
    em.op("dve", lambda e: e.tensor_copy(Af(TB_O, T)[:, :],
                                         Af(TB_O, T)[:, :].bitcast(I32)))
    em.op("dve", lambda e: e.scalar_tensor_tensor(Af(TA_O, T)[:, :],
                                                  Af(TB_O, T)[:, :], -TWO_PI,
                                                  Af(TA_O, T)[:, :],
                                                  ALU.mult, ALU.add))
    em.op("dve", lambda e: e.tensor_scalar(Af(TB_O, T)[:, :], Af(TA_O, T)[:, :],
                                           math.pi, None, ALU.is_gt))
    em.op("dve", lambda e: e.scalar_tensor_tensor(Af(TA_O, T)[:, :],
                                                  Af(TB_O, T)[:, :], -TWO_PI,
                                                  Af(TA_O, T)[:, :],
                                                  ALU.mult, ALU.add),
          tok="mready")
    em.op("act", lambda e: e.activation(Af(TB_O, T)[:, :], Af(TA_O, T)[:, :],
                                        AF.Sin), waits=["mready"], tok="sinm")
    em.op("act", lambda e: e.activation(Af(TA_O, T)[:, :], Af(TA_O, T)[:, :],
                                        AF.Sin, scale=0.5), tok="sinh")
    em.op("dve", lambda e: e.tensor_scalar(SINPM[:, :], Af(TB_O, T)[:, :],
                                           pcol[:, 3:4], None, ALU.mult),
          waits=["sinm"])
    em.op("dve", lambda e: e.tensor_mul(Af(TA_O, T)[:, :], Af(TA_O, T)[:, :],
                                        Af(TA_O, T)[:, :]), waits=["sinh"])
    em.op("dve", lambda e: e.tensor_scalar(COS[:, :], Af(TA_O, T)[:, :], -2.0, 1.0,
                                           ALU.mult, ALU.add), tok="trig")

    VS = Af(VS_O, 1024)
    em.dma("sync", VS[0:8, 0:128],
           wf["ln1_g"].rearrange("(a b) -> a b", b=128), waits=["trig"],
           tok="ld_ln1g", sem="vsq")
    em.dma("sync", VS[8:16, 0:128], wf["ln1_b"].rearrange("(a b) -> a b", b=128),
           tok="ld_ln1b", sem="vsq")
    em.dma("sync", VS[16:24, 0:128], wf["ln2_g"].rearrange("(a b) -> a b", b=128),
           tok="ld_ln2g", sem="vsq")
    em.dma("sync", VS[24:32, 0:128], wf["ln2_b"].rearrange("(a b) -> a b", b=128),
           tok="ld_ln2b", sem="vsq")
    em.dma("sync", VS[32:40, 0:128], wf["b2"].rearrange("(a b) -> a b", b=128),
           tok="ld_b2", sem="vsq")
    em.waits_only("pe", ["ld_ln1g", "ld_ln1b", "ld_ln2g", "ld_ln2b", "ld_b2",
                         "trig"])
    em.tr(PA[:, 0:40], VS[0:40, 0:128], identf[0:40, 0:40], tok="tr_lnv")
    for i, dst in enumerate((ln1g, ln1b_, ln2g, ln2b_, b2c)):
        em.op("dve", lambda e, dst=dst, i=i: e.tensor_copy(
            dst[:, :], PA[:, 8 * i:8 * i + 8]),
            waits=["tr_lnv"], tok=f"lncol{i}")
    em.dma("sync", VS[0:32, 0:128], wf["b1"].rearrange("(a b) -> a b", b=128),
           waits=["lncol4"], tok="ld_b1", sem="vsq")
    em.waits_only("pe", ["ld_b1", "lncol4"])
    em.tr(PA[:, 8:40], VS[0:32, 0:128], identf[0:32, 0:32], tok="tr_b1")
    em.op("dve", lambda e: e.tensor_copy(b1c[:, :], PA[:, 8:40]), waits=["tr_b1"],
          tok="b1ready")
    em.dma("sync", VS[0:16, 0:1024], wf["gate_w"], waits=["b1ready"],
           tok="ld_gw", sem="vsq")
    for kc in range(DC):
        em.tr(PB[:, kc * 16:(kc + 1) * 16], VS[0:16, kc * 128:(kc + 1) * 128],
              identf[0:16, 0:16],
              waits=["ld_gw"] if kc == 0 else (), tok=f"tr_gw{kc}")
    em.op("dve", lambda e: e.tensor_copy(gwT[:, :], PB[:, 0:128]),
          waits=[f"tr_gw{DC-1}"], tok="gwT")

    pc_tok = {}
    for name in scr:
        src = {"Wqkv": wf["Wqkv"], "Wo": wf["Wo"], "W1": wf["W1"],
               "W2": wf["W2"], "ws_gate": wf["ws_gate"], "ws_up": wf["ws_up"],
               "ws_down": wf["ws_down"]}.get(name)
        if src is None:
            e_ = int(name[2:])
            src = {"wg": wf["w_gate"], "wu": wf["w_up"],
                   "wd": wf["w_down"]}[name[:2]] [e_]
        KC, Mtot, mw, nmg = scr_meta[name]
        gsem = f"pq_e{name[2:]}" if name[:2] in ("wg", "wu", "wd") else f"pq_{name}"
        for mg in range(nmg):
            em.dma("pool",
                   scr[name][mg].rearrange("p (kc m) -> p kc m", m=mw),
                   src[:, mg * mw:(mg + 1) * mw].rearrange(
                       "(kc p) m -> p kc m", p=128),
                   tok=f"pc_{name}_{mg}" if mg == nmg - 1 else None, sem=gsem)
        pc_tok[name] = f"pc_{name}_{nmg-1}"
    for e_ in range(NE):
        pc_tok[f"wg{e_}"] = pc_tok[f"wd{e_}"]
        pc_tok[f"wu{e_}"] = pc_tok[f"wd{e_}"]

    def wload(off, half, name, mg, waits=(), tag="", stream="kw"):
        KC, Mtot, mw, nmg = scr_meta[name]
        w_cols = KC * mw
        dst = A(off + half * w_cols, w_cols)
        em.dma("sync", dst, scr[name][mg], waits=list(waits) + [pc_tok[name]],
               tok="w_" + tag, sem=f"ws_{stream}{half}")
        return "w_" + tag

    def wtile(off, half, name, kc, mcol, w):
        KC, Mtot, mw, nmg = scr_meta[name]
        w_cols = KC * mw
        base = off + half * w_cols + kc * mw + mcol
        return ARENA[:, base:base + w]

    prev_barrier = []

    for s in range(nseq):
        sfx = f"s{s}"
        HHs = lambda c, t0, n: A(HH_O + c * T + t0, n)
        Rs = lambda c, t0, n: A(R_O + c * T + t0, n)
        KRs = lambda c, t0, n: A(KR_O + c * T + t0, n)

        def ld_slices(ct):
            p = ct % 2
            xdst = Af(XT_O, 2 * D)[:, p * D:(p + 1) * D]
            htok = Af(HT_O + p * 2048, D)
            ss = [stat[:, 8 * p + k:8 * p + k + 1] for k in range(6)]
            return p, xdst, htok, ss

        def ld_stage1(ct):
            p, xdst, htok, (s0, s1, s2, s3, s4, s5) = ld_slices(ct)
            wl = list(prev_barrier) if ct == 0 else []
            if ct == 0 and s == 0:
                wl += ["gwT", "trig"]
            if ct >= 2:
                wl.append(f"{sfx}_ld{ct-2}_done")
            em.dma("sync", xdst, x_h[s * T + ct * 128: s * T + (ct + 1) * 128, :],
                   waits=wl, tok=f"{sfx}_x{ct}", sem=f"xq{ct % 2}")
            em.op("dve", lambda e, xdst=xdst, s0=s0: e.reduce_sum(s0, xdst, AX.X),
                  waits=[f"{sfx}_x{ct}"] + (list(prev_barrier) if ct == 0 else []))
            em.op("dve", lambda e, s0=s0: e.tensor_scalar(s0, s0,
                                                          1.0 / D, None, ALU.mult),
                  tok=f"{sfx}_mu0_{ct}")
            sq_w = [f"{sfx}_x{ct}"]
            if ct >= 2:
                sq_w += [f"{sfx}_trh{ct-2}", f"{sfx}_var2_{ct-2}"]
            else:
                sq_w += ["trig"] + (list(prev_barrier) if s else [])
            em.op("act", lambda e, xdst=xdst, htok=htok, s1=s1: e.activation(
                htok[:, :], xdst, AF.Square, accum_out=s1),
                waits=sq_w, tok=f"{sfx}_sq0_{ct}")
            em.op("dve", lambda e, s0=s0, s2=s2: e.tensor_mul(s2, s0, s0),
                  waits=[f"{sfx}_mu0_{ct}", f"{sfx}_sq0_{ct}"])
            em.op("dve", lambda e, s1=s1, s2=s2: e.scalar_tensor_tensor(
                s2, s1, 1.0 / D, s2, ALU.mult,
                ALU.subtract), tok=f"{sfx}_var0_{ct}")
            em.op("act", lambda e, s2=s2: e.activation(s2, s2, AF.Ln,
                                                       bias=consts[:, 0:1]),
                  waits=[f"{sfx}_var0_{ct}"])
            em.op("act", lambda e, s2=s2: e.activation(s2, s2, AF.Exp,
                                                       scale=-0.5),
                  tok=f"{sfx}_r0_{ct}")
            em.op("dve", lambda e, xdst=xdst, htok=htok, s0=s0, s2=s2:
                  e.tensor_scalar(htok[:, :], xdst, s0, s2, ALU.subtract,
                                  ALU.mult),
                  waits=[f"{sfx}_r0_{ct}"], tok=f"{sfx}_h0t_{ct}")

        def ld_stage2(ct):
            p, xdst, htok, (s0, s1, s2, s3, s4, s5) = ld_slices(ct)
            em.op("dve", lambda e, htok=htok, s3=s3: e.reduce_sum(s3, htok[:, :],
                                                                  AX.X),
                  waits=[f"{sfx}_trmu{ct-2}"] if ct >= 2 else ())
            em.op("dve", lambda e, s3=s3: e.tensor_scalar(s3, s3,
                                                          1.0 / D, None, ALU.mult),
                  tok=f"{sfx}_mu2_{ct}")
            em.op("act", lambda e, htok=htok, s4=s4: e.activation(
                sgf[:, :], htok[:, 0:512], AF.Square, accum_out=s4),
                waits=[f"{sfx}_h0t_{ct}"] +
                      ([f"{sfx}_trr{ct-2}"] if ct >= 2 else []))
            em.op("act", lambda e, htok=htok, s5=s5: e.activation(
                z1f[:, :], htok[:, 512:1024], AF.Square, accum_out=s5),
                tok=f"{sfx}_sq2_{ct}")
            em.op("dve", lambda e, s4=s4, s5=s5: e.tensor_add(s4, s4, s5),
                  waits=[f"{sfx}_mu2_{ct}", f"{sfx}_sq2_{ct}"])
            em.op("dve", lambda e, s3=s3, s5=s5: e.tensor_mul(s5, s3, s3))
            em.op("dve", lambda e, s4=s4, s5=s5: e.scalar_tensor_tensor(
                s5, s4, 1.0 / D, s5, ALU.mult,
                ALU.subtract), tok=f"{sfx}_var2_{ct}")
            em.op("act", lambda e, s5=s5: e.activation(s5, s5, AF.Ln,
                                                       bias=consts[:, 0:1]),
                  waits=[f"{sfx}_var2_{ct}"])
            em.op("act", lambda e, s5=s5: e.activation(s5, s5, AF.Exp,
                                                       scale=-0.5),
                  tok=f"{sfx}_r2c_{ct}")
            em.waits_only("pe", [f"{sfx}_h0t_{ct}", f"{sfx}_r2c_{ct}"])
            if ct == 0 and s == 0:
                em.waits_only("pe", ["gwT"])
            if ct >= 1:
                em.waits_only("pe", [f"{sfx}_ld{ct-1}_done", f"{sfx}_rows{ct-1}"])
            for c in range(DC):
                em.tr(PA[:, c * 128:(c + 1) * 128],
                      xdst[:, c * 128:(c + 1) * 128], identf[:, :],
                      tok=f"{sfx}_trx{ct}" if c == DC - 1 else None)
                em.tr(PB[:, c * 128:(c + 1) * 128],
                      htok[:, c * 128:(c + 1) * 128], identf[:, :],
                      tok=f"{sfx}_trh{ct}" if c == DC - 1 else None)
            PDx = PD if ct % 2 == 0 else PEb
            em.tr(PDx[0:1, 0:128], s3, identf[:, :],
                  tok=f"{sfx}_trmu{ct}")
            em.tr(PDx[0:1, 128:256], s5, identf[:, :],
                  tok=f"{sfx}_trr{ct}")
            hhd = A(HH_O, DC * T).rearrange("p (c x) -> p c x", x=T)[
                :, :, ct * 128:(ct + 1) * 128]
            rrd = A(R_O, DC * T).rearrange("p (c x) -> p c x", x=T)[
                :, :, ct * 128:(ct + 1) * 128]
            em.op("dve", lambda e, hhd=hhd: e.tensor_copy(
                hhd, PB[:, :].rearrange("p (c x) -> p c x", x=128)),
                waits=[f"{sfx}_trh{ct}"])
            em.op("dve", lambda e, rrd=rrd, hhd=hhd: e.tensor_tensor(
                rrd, PA[:, :].rearrange("p (c x) -> p c x", x=128), hhd,
                ALU.add),
                waits=[f"{sfx}_trx{ct}"], tok=f"{sfx}_ld{ct}_done")
            em.op("dve", lambda e, PDx=PDx: e.tensor_copy(
                rowb[0:1, ct * 128:(ct + 1) * 128], PDx[0:1, 0:128]),
                waits=[f"{sfx}_trr{ct}"])
            em.op("dve", lambda e, PDx=PDx: e.tensor_copy(
                rowb[0:1, T + ct * 128: T + (ct + 1) * 128],
                PDx[0:1, 128:256]),
                waits=[f"{sfx}_trr{ct}"], tok=f"{sfx}_rows{ct}")

        ld_stage1(0)
        for ct in range(1, NKT):
            ld_stage1(ct)
            ld_stage2(ct - 1)
        ld_stage2(NKT - 1)
        dbg_store(f"{sfx}_h0", A(HH_O, DC * T), [f"{sfx}_ld{NKT-1}_done"])
        dbg_store(f"{sfx}_R0", A(R_O, DC * T), [f"{sfx}_ld{NKT-1}_done"])

        def h1_chunk(h1off, tc, pc_guard, tag):
            em.mm(PC[:, 0:QC], ones1b[0:1, :], rowb[0:1, tc * QC:(tc + 1) * QC],
                  waits=list(pc_guard) + [f"{sfx}_rows{NKT-1}"],
                  tok=f"{sfx}_bmu_{tag}")
            em.mm(PC[:, 512:512 + QC], ones1b[0:1, :],
                  rowb[0:1, T + tc * QC: T + (tc + 1) * QC],
                  tok=f"{sfx}_br_{tag}")
            em.op("dve", lambda e: e.tensor_copy(bc1[:, :], PC[:, 0:QC]),
                  waits=[f"{sfx}_bmu_{tag}"])
            em.op("dve", lambda e: e.tensor_copy(bc2[:, :], PC[:, 512:512 + QC]),
                  waits=[f"{sfx}_br_{tag}"], tok=f"{sfx}_bc_{tag}")
            for c in range(DC):
                dst = A(h1off + c * QC, QC)
                em.op("dve", lambda e, dst=dst, c=c: e.tensor_tensor(
                    dst, HHs(c, tc * QC, QC), bc1[:, :], ALU.subtract),
                    waits=[f"{sfx}_bc_{tag}"])
                em.op("dve", lambda e, dst=dst: e.tensor_tensor(
                    dst, dst, bc2[:, :], ALU.mult))
                em.op("dve", lambda e, dst=dst, c=c: e.tensor_scalar(
                    dst, dst, ln1g[:, c:c + 1], ln1b_[:, c:c + 1],
                    ALU.mult, ALU.add),
                    tok=f"{sfx}_h1_{tag}" if c == DC - 1 else None)

        if "kv" not in phases:
            continue
        dbg_fence()
        vt4 = ARENA[:, VT_O:VT_O + NKT * H * 65].rearrange(
            "p (kt h w) -> p kt h w", h=H, w=65)
        em.op("pool", lambda e: e.memset(vt4[:, :, :, 64:65], 1.0),
              waits=list(prev_barrier), tok=f"{sfx}_vones")
        for step in range(NQC):
            tc = qc = step
            pcg = [f"{sfx}_wo_pf_{tc-1}"] if tc else []
            h1_chunk(OTH_O, tc, pcg, f"kv{tc}")
            for wnd in range(4):
                half = wnd % 2
                if wnd >= 2:
                    kww = [f"{sfx}_kmm{tc}_{(wnd-2)*2+1}"]
                elif tc:
                    kww = [f"{sfx}_vmm{tc-1}_{wnd+2}_3"]
                else:
                    kww = list(prev_barrier)
                wl = wload(KW_O, half, "Wqkv", 4 + wnd, waits=kww,
                           tag=f"{sfx}_k{tc}_{wnd}")
                for mc in range(2):
                    mcol_g = wnd * 2 + mc
                    pbank, po = (PA, (mcol_g % 2) * 512) if mcol_g < 4 else \
                                (PB, (mcol_g % 2) * 512)
                    for kc in range(DC):
                        if kc == 0 and mcol_g >= 2:
                            em.waits_only("pe", [f"{sfx}_kev{tc}_{mcol_g-2}"])
                        kmw = ()
                        if kc == 0:
                            kmw = [wl, f"{sfx}_h1_kv{tc}"]
                            if mcol_g == 0 and tc >= 1:
                                kmw.append(f"{sfx}_ex{tc-1}_7_{4*tc-1}")
                        em.mm(pbank[:, po:po + QC],
                              wtile(KW_O, half, "Wqkv", kc, mc * 128, 128),
                              A(OTH_O + kc * QC, QC),
                              start=(kc == 0), stop=(kc == DC - 1),
                              waits=kmw,
                              tok=f"{sfx}_kmm{tc}_{mcol_g}" if kc == DC - 1
                                  else None)
                    em.op("dve", lambda e, mcol_g=mcol_g, pbank=pbank, po=po:
                          e.tensor_copy(KRs(mcol_g, tc * QC, QC),
                                        pbank[:, po:po + QC]),
                          waits=[f"{sfx}_kmm{tc}_{mcol_g}"],
                          tok=f"{sfx}_kev{tc}_{mcol_g}")
            for wnd in range(4):
                half = wnd % 2
                vww = ([f"{sfx}_vmm{tc}_{wnd-2}_3"] if wnd >= 2 else
                       [f"{sfx}_kmm{tc}_{(wnd+2)*2+1}"])
                wl = wload(KW_O, half, "Wqkv", 8 + wnd, waits=vww,
                           tag=f"{sfx}_v{tc}_{wnd}")
                for tt in range(4):
                    po = (tt % 2) * 512
                    for kc in range(DC):
                        if kc == 0 and tt >= 2:
                            em.waits_only("pe", [f"{sfx}_vev{tc}_{wnd}_{tt-2}"])
                        elif kc == 0 and wnd >= 1:
                            em.waits_only("pe", [f"{sfx}_vev{tc}_{wnd-1}_{tt+2}"])
                        elif kc == 0 and tc >= 1:
                            em.waits_only("pe", [f"{sfx}_otB{tc-1}_7",
                                                 f"{sfx}_bc_kv{tc}"])
                        elif kc == 0:
                            em.waits_only("pe", [f"{sfx}_bc_kv{tc}"])
                        em.mm(PC[:, po:po + 256],
                              A(OTH_O + kc * QC + tt * 128, 128),
                              wtile(KW_O, half, "Wqkv", kc, 0, 256),
                              start=(kc == 0), stop=(kc == DC - 1),
                              waits=[wl] if kc == 0 else (),
                              tok=f"{sfx}_vmm{tc}_{wnd}_{tt}" if kc == DC - 1
                                  else None)
                    kt = tc * 4 + tt
                    hbase = wnd * 4
                    vdst = ARENA[:, VT_O + kt * (H * 65) + hbase * 65:][:, 0:4 * 65]
                    vdst3 = vdst.rearrange("p (h w) -> p h w", w=65)[:, :, 0:64]
                    em.op("dve", lambda e, vdst3=vdst3, po=po: e.tensor_copy(
                        vdst3, PC[:, po:po + 256].rearrange("p (h w) -> p h w",
                                                            w=64)),
                        waits=[f"{sfx}_vmm{tc}_{wnd}_{tt}"],
                        tok=f"{sfx}_vev{tc}_{wnd}_{tt}")
            for c in range(DC):
                ksl = KRs(c, tc * QC, QC)
                csl = COS[:, tc * QC:(tc + 1) * QC]
                ssl = SINPM[:, tc * QC:(tc + 1) * QC]
                em.op("dve", lambda e, ksl=ksl: e.stream_shuffle(sqb[:, :], ksl,
                                                                 SHUF),
                      waits=[f"{sfx}_kev{tc}_{c}", "trig"])
                em.op("dve", lambda e, ksl=ksl, csl=csl: e.tensor_tensor(
                    rpA[:, :], ksl, csl, ALU.mult))
                em.op("dve", lambda e, ssl=ssl: e.tensor_tensor(
                    rpB[:, :], sqb[:, :], ssl, ALU.mult))
                em.op("dve", lambda e, ksl=ksl: e.tensor_tensor(
                    ksl, rpA[:, :], rpB[:, :], ALU.add),
                    tok=f"{sfx}_krope{tc}_{c}")
            for c in range(DC):
                hA, hB = 2 * c, 2 * c + 1
                em.op("act", lambda e, c=c, tc=tc: e.activation(
                    knb[:, :], KRs(c, tc * QC, QC), AF.Square),
                    waits=[f"{sfx}_krope{tc}_{c}"] +
                          ([f"{sfx}_knB{c-1}_{tc}"] if c else
                           ([f"{sfx}_knB{DC-1}_{tc-1}"] if tc else [])),
                    tok=f"{sfx}_knsq{c}_{tc}")
                knw = [f"{sfx}_knsq{c}_{tc}"]
                if c:
                    knw.append(f"{sfx}_knd{c-1}_{tc}")
                elif tc:
                    knw.append(f"{sfx}_knd{DC-1}_{tc-1}")
                    knw.append(f"{sfx}_otB{tc-1}_7")
                em.mm(PD[0:1, 0:QC], onescb[0:64, 0:1], knb[0:64, :],
                      waits=knw, tok=f"{sfx}_knA{c}_{tc}")
                em.mm(PEb[0:1, 0:QC], onescb[64:128, 0:1], knb[64:128, :],
                      tok=f"{sfx}_knB{c}_{tc}")
                em.op("dve", lambda e: e.reduce_max(stat[0:1, 6:7],
                                                    PD[0:1, 0:QC], AX.X),
                      waits=[f"{sfx}_knA{c}_{tc}"])
                em.op("dve", lambda e, hA=hA, tc=tc: (
                    e.tensor_max(rowf[0:1, 1552 + hA:1553 + hA],
                                 rowf[0:1, 1552 + hA:1553 + hA], stat[0:1, 6:7])
                    if tc else
                    e.tensor_copy(rowf[0:1, 1552 + hA:1553 + hA],
                                  stat[0:1, 6:7])))
                em.op("dve", lambda e: e.reduce_max(stat[0:1, 6:7],
                                                    PEb[0:1, 0:QC], AX.X),
                      waits=[f"{sfx}_knB{c}_{tc}"])
                em.op("dve", lambda e, hB=hB, tc=tc: (
                    e.tensor_max(rowf[0:1, 1552 + hB:1553 + hB],
                                 rowf[0:1, 1552 + hB:1553 + hB], stat[0:1, 6:7])
                    if tc else
                    e.tensor_copy(rowf[0:1, 1552 + hB:1553 + hB],
                                  stat[0:1, 6:7])),
                    tok=f"{sfx}_knd{c}_{tc}")
            if step == NQC - 1:
                dbg_store(f"{sfx}_kr", A(KR_O, DC * T),
                          [f"{sfx}_krope{NQC-1}_{DC-1}"])
                dbg_store(f"{sfx}_vt", A(VT_O, NKT * H * 65),
                          [f"{sfx}_vev{NQC-1}_3_3", f"{sfx}_vones"])

            if "attn" not in phases:
                continue
            dbg_fence()
            pcg = [f"{sfx}_vev{qc}_3_3"]
            h1_chunk(QQ_O, qc, pcg, f"q{qc}")
            for mcol_g in range(8):
                half = mcol_g % 2
                if mcol_g >= 2:
                    wqw = [f"{sfx}_qmm{qc}_{mcol_g-2}"]
                elif qc:
                    wqw = [f"{sfx}_qmm{qc-1}_{mcol_g+6}"]
                else:
                    wqw = list(prev_barrier)
                wq_dst = A(WQ_O + half * 1024, 1024)
                wq_src = scr["Wqkv"][mcol_g // 2].rearrange(
                    "p (kc m) -> p kc m", m=256)[
                    :, :, (mcol_g % 2) * 128:(mcol_g % 2) * 128 + 128]
                em.dma("sync", wq_dst.rearrange("p (kc m) -> p kc m", m=128),
                       wq_src, waits=wqw + [pc_tok["Wqkv"]],
                       tok=f"{sfx}_wqld{qc}_{mcol_g}", sem=f"ws_wq{half}")
                wl = f"{sfx}_wqld{qc}_{mcol_g}"
                po = (mcol_g % 2) * 512
                for kc in range(DC):
                    if kc == 0 and mcol_g >= 2:
                        em.waits_only("pe", [f"{sfx}_qev{qc}_{mcol_g-2}"])
                    em.mm(PA[:, po:po + QC],
                          ARENA[:, WQ_O + half * 1024 + kc * 128:
                                WQ_O + half * 1024 + kc * 128 + 128],
                          A(QQ_O + kc * QC, QC),
                          start=(kc == 0), stop=(kc == DC - 1),
                          waits=([wl, f"{sfx}_h1_q{qc}"] +
                                 ([f"{sfx}_attf_{qc-1}"] if qc else []))
                                if kc == 0 else (),
                          tok=f"{sfx}_qmm{qc}_{mcol_g}" if kc == DC - 1
                              else None)
                em.op("dve", lambda e, mcol_g=mcol_g, po=po: e.tensor_copy(
                    A(OTH_O + mcol_g * QC, QC), PA[:, po:po + QC]),
                    waits=[f"{sfx}_qmm{qc}_{mcol_g}"],
                    tok=f"{sfx}_qev{qc}_{mcol_g}")
            csl = COS[:, qc * QC:(qc + 1) * QC]
            ssl = SINPM[:, qc * QC:(qc + 1) * QC]
            for c in range(DC):
                src = A(OTH_O + c * QC, QC)
                dst = A(QQ_O + c * QC, QC)
                em.op("dve", lambda e, src=src: e.stream_shuffle(sqb[:, :], src,
                                                                 SHUF),
                      waits=[f"{sfx}_qev{qc}_{c}"])
                em.op("dve", lambda e, src=src, csl=csl: e.tensor_tensor(
                    rpA[:, :], src, csl, ALU.mult))
                em.op("dve", lambda e, ssl=ssl: e.tensor_tensor(
                    rpB[:, :], sqb[:, :], ssl, ALU.mult))
                em.op("dve", lambda e, dst=dst: e.tensor_tensor(
                    dst, rpA[:, :], rpB[:, :], ALU.add),
                    tok=f"{sfx}_qrope{qc}_{c}")
            for c in range(DC):
                hA, hB = 2 * c, 2 * c + 1
                em.op("act", lambda e, c=c: e.activation(
                    knb[:, :], A(QQ_O + c * QC, QC), AF.Square),
                    waits=[f"{sfx}_qrope{qc}_{c}"] +
                          ([f"{sfx}_qnB{qc}_{c-1}"] if c else
                           [f"{sfx}_knB{DC-1}_{tc}"]),
                    tok=f"{sfx}_qnsq{qc}_{c}")
                qnw = [f"{sfx}_qnsq{qc}_{c}"]
                if c:
                    qnw.append(f"{sfx}_qnd{qc}_{c-1}")
                else:
                    qnw.append(f"{sfx}_knd{DC-1}_{tc}")
                em.mm(PD[0:1, 0:QC], onescb[0:64, 0:1], knb[0:64, :],
                      waits=qnw, tok=f"{sfx}_qnA{qc}_{c}")
                em.mm(PEb[0:1, 0:QC], onescb[64:128, 0:1], knb[64:128, :],
                      tok=f"{sfx}_qnB{qc}_{c}")
                em.op("dve", lambda e: e.reduce_max(stat[0:1, 6:7],
                                                    PD[0:1, 0:QC], AX.X),
                      waits=[f"{sfx}_qnA{qc}_{c}"])
                em.op("dve", lambda e, hA=hA: e.tensor_copy(
                    rowf[0:1, 1536 + hA:1537 + hA], stat[0:1, 6:7]))
                em.op("dve", lambda e: e.reduce_max(stat[0:1, 6:7],
                                                    PEb[0:1, 0:QC], AX.X),
                      waits=[f"{sfx}_qnB{qc}_{c}"])
                em.op("dve", lambda e, hB=hB: e.tensor_copy(
                    rowf[0:1, 1536 + hB:1537 + hB], stat[0:1, 6:7]),
                    tok=f"{sfx}_qnd{qc}_{c}")
            em.op("dve", lambda e: e.tensor_tensor(rowf[0:1, 1568:1584],
                                                   rowf[0:1, 1536:1552],
                                                   rowf[0:1, 1552:1568],
                                                   ALU.mult),
                  waits=[f"{sfx}_qnd{qc}_{DC-1}"] +
                        ([f"{sfx}_ex{qc-1}_7_{4*qc-1}"] if qc else []),
                  tok=f"{sfx}_cprod{qc}")
            em.op("act", lambda e: e.activation(rowf[0:1, 1568:1584],
                                                rowf[0:1, 1568:1584], AF.Ln,
                                                bias=consts[0:1, 0:1]),
                  waits=[f"{sfx}_cprod{qc}"])
            em.op("act", lambda e: e.activation(rowf[0:1, 1568:1584],
                                                rowf[0:1, 1568:1584], AF.Exp,
                                                scale=0.5),
                  tok=f"{sfx}_crow{qc}")
            em.op("dve", lambda e: e.tensor_scalar(rowb[0:1, 2 * T:2 * T + 16],
                                                   rowf[0:1, 1568:1584],
                                                   -0.125, None, ALU.mult),
                  waits=[f"{sfx}_crow{qc}"], tok=f"{sfx}_crowb{qc}")
            em.mm(PD[:, 0:16], ones1b[0:1, :], rowb[0:1, 2 * T:2 * T + 16],
                  waits=[f"{sfx}_crowb{qc}"], tok=f"{sfx}_cbm{qc}")
            em.op("dve", lambda e: e.tensor_copy(cnegc[:, :], PD[:, 0:16]),
                  waits=[f"{sfx}_cbm{qc}"])
            cv = cnegc[:, :].rearrange("p (a b) -> p a b", b=2)
            em.op("dve", lambda e, cv=cv: e.tensor_tensor(
                cnegP[:, :], cv[:, :, 0], cv[:, :, 1], ALU.min),
                tok=f"{sfx}_cnegP{qc}")
            if qc == 0:
                dbg_store(f"{sfx}_cneg", cnegc[:, :], [f"{sfx}_cnegP{qc}"])
                dbg_store(f"{sfx}_q0", A(QQ_O, DC * QC),
                          [f"{sfx}_qrope0_{DC-1}"])
            NKTq = 4 * qc + 4
            ps4v = PS4.rearrange("p (g s x) -> p g s x", g=2, s=2, x=512)
            for pr in range(8):
                hA, hB = 2 * pr, 2 * pr + 1
                qA = ARENA[0:64, QQ_O + pr * QC: QQ_O + (pr + 1) * QC]
                qB = ARENA[64:128, QQ_O + pr * QC: QQ_O + (pr + 1) * QC]

                def emit_av(kt, pr=pr, hA=hA, hB=hB):
                    p = kt % 2
                    diag = (kt >= 4 * qc)
                    vA = ARENA[:, VT_O + kt * (H * 65) + hA * 65:
                               VT_O + kt * (H * 65) + hA * 65 + 65]
                    vB = ARENA[:, VT_O + kt * (H * 65) + hB * 65:
                               VT_O + kt * (H * 65) + hB * 65 + 65]
                    wl = ([f"{sfx}_mk{qc}_{pr}_{kt}"] if diag else
                          [f"{sfx}_ex{qc}_{pr}_{kt}"])
                    if kt == 0:
                        wl.append(f"{sfx}_vones")
                        if pr >= 1:
                            wl.append(f"{sfx}_otB{qc}_{pr-1}")
                        else:
                            wl.append(f"{sfx}_qnd{qc}_{DC-1}")
                            if qc >= 1:
                                wl.append(f"{sfx}_otB{qc-1}_7")
                    em.mm(PD[0:65, :], vA, A(EA_O + p * 1024, 512),
                          start=(kt == 0), stop=(kt == NKTq - 1), waits=wl)
                    em.mm(PEb[0:65, :], vB, A(EA_O + p * 1024 + 512, 512),
                          start=(kt == 0), stop=(kt == NKTq - 1),
                          tok=f"{sfx}_av{qc}_{pr}_{kt}")

                for b in range(NKTq // 2):
                    for kt in (2 * b, 2 * b + 1):
                        p = kt % 2
                        kA = ARENA[0:64, KR_O + pr * T + kt * 128:
                                   KR_O + pr * T + (kt + 1) * 128]
                        kB = ARENA[64:128, KR_O + pr * T + kt * 128:
                                   KR_O + pr * T + (kt + 1) * 128]
                        wl = []
                        if kt == 0:
                            wl.append(f"{sfx}_qrope{qc}_{pr}")
                        if kt >= 2:
                            wl.append(f"{sfx}_ex{qc}_{pr}_{kt-2}")
                        elif pr >= 1:
                            wl.append(f"{sfx}_ex{qc}_{pr-1}_{4*qc+3}")
                        elif qc >= 1:
                            wl.append(f"{sfx}_ex{qc-1}_7_{4*qc-1}")
                        em.mm(PS4[:, p * 512: p * 512 + QC], kA, qA, waits=wl)
                        em.mm(PS4[:, 1024 + p * 512: 1024 + p * 512 + QC],
                              kB, qB, tok=f"{sfx}_s{qc}_{pr}_{kt}")
                    if b >= 1:
                        emit_av(2 * b - 2)
                        emit_av(2 * b - 1)
                    for kt in (2 * b, 2 * b + 1):
                        p = kt % 2
                        ewl = [f"{sfx}_s{qc}_{pr}_{kt}", f"{sfx}_cnegP{qc}"]
                        if kt >= 2:
                            ewl.append(f"{sfx}_av{qc}_{pr}_{kt-2}")
                        elif pr >= 1:
                            ewl.append(f"{sfx}_av{qc}_{pr-1}_{4*qc+3}")
                        elif qc >= 1:
                            ewl.append(f"{sfx}_av{qc-1}_7_{4*qc-1}")
                        em.op("act", lambda e, p=p, pr=pr: e.activation(
                            A(EA_O + p * 1024, 1024).rearrange(
                                "q (g x) -> q g x", g=2),
                            ps4v[:, :, p, :], AF.Exp,
                            bias=cnegP[:, pr:pr + 1], scale=0.125),
                            waits=ewl, tok=f"{sfx}_ex{qc}_{pr}_{kt}")
                        if kt >= 4 * qc:
                            r = kt * 128 - qc * 512
                            em.op("dve", lambda e, p=p, r=r: e.tensor_tensor(
                                A(EA_O + p * 1024, 512),
                                A(EA_O + p * 1024, 512),
                                maskB[:, 384 - r:896 - r], ALU.mult),
                                waits=[f"{sfx}_ex{qc}_{pr}_{kt}"])
                            em.op("dve", lambda e, p=p, r=r: e.tensor_tensor(
                                A(EA_O + p * 1024 + 512, 512),
                                A(EA_O + p * 1024 + 512, 512),
                                maskB[:, 384 - r:896 - r], ALU.mult),
                                tok=f"{sfx}_mk{qc}_{pr}_{kt}")
                emit_av(NKTq - 2)
                emit_av(NKTq - 1)
                em.op("dve", lambda e: e.tensor_copy(lrow[0:1, 0:512],
                                                     PD[64:65, :]),
                      waits=[f"{sfx}_av{qc}_{pr}_{NKTq-1}"])
                em.op("dve", lambda e: e.tensor_copy(lrow[0:1, 512:1024],
                                                     PEb[64:65, :]),
                      tok=f"{sfx}_lr{qc}_{pr}")
                em.mm(PC[0:64, 0:512], ones1b[0:1, 0:64], lrow[0:1, 0:512],
                      waits=[f"{sfx}_lr{qc}_{pr}"])
                em.mm(PC[64:128, 0:512], ones1b[0:1, 0:64], lrow[0:1, 512:1024],
                      tok=f"{sfx}_lb{qc}_{pr}")
                em.op("dve", lambda e: e.reciprocal(sgf[:, :], PC[:, 0:512]),
                      waits=[f"{sfx}_lb{qc}_{pr}"], tok=f"{sfx}_rcp{qc}_{pr}")
                cbase = pr * QC
                em.op("dve", lambda e, cbase=cbase: e.tensor_tensor(
                    ARENA[0:64, OTH_O + cbase: OTH_O + cbase + QC],
                    PD[0:64, :], sgf[0:64, :], ALU.mult),
                    waits=[f"{sfx}_rcp{qc}_{pr}"],
                    tok=f"{sfx}_otA{qc}_{pr}")
                em.op("dve", lambda e, cbase=cbase: e.tensor_tensor(
                    ARENA[64:128, OTH_O + cbase: OTH_O + cbase + QC],
                    PEb[0:64, :], sgf[64:128, :], ALU.mult),
                    tok=f"{sfx}_otB{qc}_{pr}")
            if qc == 0:
                dbg_store(f"{sfx}_ot0", A(OTH_O, DC * QC), [f"{sfx}_otB{qc}_7"])

            for wnd in range(8):
                half = wnd % 2
                if wnd >= 2:
                    wow = [f"{sfx}_womm{qc}_{wnd-2}"]
                elif qc:
                    wow = [f"{sfx}_womm{qc-1}_{wnd+6}"]
                else:
                    wow = list(prev_barrier)
                wl = wload(WO_O, half, "Wo", wnd, stream="wo", waits=wow,
                           tag=f"{sfx}_wo{qc}_{wnd}")
                po = (wnd % 2) * 512
                for kc in range(DC):
                    if kc == 0 and wnd >= 2:
                        em.waits_only("pe", [f"{sfx}_woR{qc}_{wnd-2}"])
                    em.mm(PC[:, po:po + QC],
                          wtile(WO_O, half, "Wo", kc, 0, 128),
                          A(OTH_O + kc * QC, QC),
                          start=(kc == 0), stop=(kc == DC - 1),
                          waits=[wl, f"{sfx}_otB{qc}_7", f"{sfx}_otA{qc}_7"]
                                if kc == 0 else (),
                          tok=f"{sfx}_womm{qc}_{wnd}" if kc == DC - 1 else None)
                em.op("dve", lambda e, wnd=wnd, po=po: e.tensor_tensor(
                    HHs(wnd, qc * QC, QC), HHs(wnd, qc * QC, QC),
                    PC[:, po:po + QC], ALU.add),
                    waits=[f"{sfx}_womm{qc}_{wnd}"],
                    tok=f"{sfx}_wox{qc}_{wnd}")
                em.op("dve", lambda e, wnd=wnd, po=po: e.tensor_tensor(
                    Rs(wnd, qc * QC, QC), Rs(wnd, qc * QC, QC),
                    PC[:, po:po + QC], ALU.add),
                    tok=f"{sfx}_woR{qc}_{wnd}")
            em.op("dve", lambda e: e.memset(stat[:, 7:8], 0.0),
                  waits=[f"{sfx}_woR{qc}_7"], tok=f"{sfx}_wo_pf_{qc}")
            em.op("dve", lambda e: e.memset(stat[:, 7:8], 0.0),
                  waits=[f"{sfx}_woR{qc}_7", f"{sfx}_otB{qc}_7"],
                  tok=f"{sfx}_attf_{qc}")
        dbg_store(f"{sfx}_x1", A(HH_O, DC * T), [f"{sfx}_wo_pf_{NQC-1}"])

        def ln_rows(xoff, tt, guard_mm, guard_sq, tag):
            for c in range(DC):
                xsl = A(xoff + c * T + tt * QC, QC)
                em.mm(PD[0:1, 0:QC], onescb[:, 0:1], xsl,
                      start=(c == 0), stop=(c == DC - 1),
                      waits=guard_mm if c == 0 else (),
                      tok=f"{tag}_sum" if c == DC - 1 else None)
            for c in range(DC):
                xsl = A(xoff + c * T + tt * QC, QC)
                em.op("act", lambda e, xsl=xsl: e.activation(sqb[:, :], xsl,
                                                             AF.Square),
                      waits=guard_sq + ([f"{tag}_ss{c-1}"] if c else []),
                      tok=f"{tag}_sq{c}")
                em.mm(PEb[0:1, 0:QC], onescb[:, 0:1], sqb[:, :],
                      start=(c == 0), stop=(c == DC - 1),
                      waits=[f"{tag}_sq{c}"],
                      tok=f"{tag}_ss{c}")
            em.op("dve", lambda e: e.tensor_scalar(rowf[0:1, 512:512 + QC],
                                                   PD[0:1, 0:QC], 1.0 / D, None,
                                                   ALU.mult),
                  waits=[f"{tag}_sum"])
            em.op("dve", lambda e: e.tensor_mul(rowf[0:1, 1024:1024 + QC],
                                                rowf[0:1, 512:512 + QC],
                                                rowf[0:1, 512:512 + QC]))
            em.op("dve", lambda e: e.scalar_tensor_tensor(
                rowf[0:1, 1024:1024 + QC], PEb[0:1, 0:QC], 1.0 / D,
                rowf[0:1, 1024:1024 + QC], ALU.mult, ALU.subtract),
                waits=[f"{tag}_ss{DC-1}"], tok=f"{tag}_var")
            em.op("act", lambda e: e.activation(rowf[0:1, 1024:1024 + QC],
                                                rowf[0:1, 1024:1024 + QC], AF.Ln,
                                                bias=consts[0:1, 0:1]),
                  waits=[f"{tag}_var"])
            em.op("act", lambda e: e.activation(rowf[0:1, 1024:1024 + QC],
                                                rowf[0:1, 1024:1024 + QC],
                                                AF.Exp, scale=-0.5),
                  tok=f"{tag}_r")
            em.op("dve", lambda e: e.tensor_copy(rowb[0:1, 2 * T + 64:
                                                      2 * T + 64 + QC],
                                                 rowf[0:1, 512:512 + QC]),
                  waits=[f"{tag}_r"])
            em.op("dve", lambda e: e.tensor_copy(rowb[0:1, 2 * T + 640:
                                                      2 * T + 640 + QC],
                                                 rowf[0:1, 1024:1024 + QC]),
                  tok=f"{tag}_rows")
            em.mm(PC[:, 0:QC], ones1b[0:1, :],
                  rowb[0:1, 2 * T + 64:2 * T + 64 + QC],
                  waits=[f"{tag}_rows"], tok=f"{tag}_bmu")
            em.mm(PC[:, 512:512 + QC], ones1b[0:1, :],
                  rowb[0:1, 2 * T + 640:2 * T + 640 + QC], tok=f"{tag}_br")
            em.op("dve", lambda e: e.tensor_copy(bc1[:, :], PC[:, 0:QC]),
                  waits=[f"{tag}_bmu"])
            em.op("dve", lambda e: e.tensor_copy(bc2[:, :], PC[:, 512:512 + QC]),
                  waits=[f"{tag}_br"], tok=f"{tag}_bc")

        if "ffn" not in phases:
            continue
        for tt in range(NQC):
            dbg_fence()
            tag = f"{sfx}_f{tt}"
            gmm = [f"{sfx}_wo_pf_{NQC-1}"]
            gsq = [f"{sfx}_attf_{NQC-1}"]
            if tt:
                gmm.append(f"{sfx}_f{tt-1}_var")
                gsq.append(f"{sfx}_f{tt-1}_ss{DC-1}")
            ln_rows(HH_O, tt, gmm, gsq, tag)
            for c in range(DC):
                dst = A(H1F_O + c * QC, QC)
                em.op("dve", lambda e, dst=dst, c=c: e.tensor_tensor(
                    dst, HHs(c, tt * QC, QC), bc1[:, :], ALU.subtract),
                    waits=[f"{tag}_bc"])
                em.op("dve", lambda e, dst=dst: e.tensor_tensor(
                    dst, dst, bc2[:, :], ALU.mult))
                em.op("dve", lambda e, dst=dst, c=c: e.tensor_scalar(
                    dst, dst, ln2g[:, c:c + 1], ln2b_[:, c:c + 1],
                    ALU.mult, ALU.add),
                    tok=f"{tag}_h2" if c == DC - 1 else None)
            for wnd in range(8):
                half = wnd % 2
                wl = wload(W1_O, half, "W1", wnd, stream="w1",
                           waits=[f"{tag}_gel{wnd-2}_3"] if wnd >= 2 else
                                 ([f"{sfx}_f{tt-1}_w2f"] if tt else
                                  [f"{sfx}_attf_{NQC-1}"]),
                           tag=f"{tag}_w1_{wnd}")
                for mc in range(4):
                    mcol_g = wnd * 4 + mc
                    po = (mc % 2) * 512
                    for kc in range(DC):
                        if kc == 0 and mc >= 2:
                            em.waits_only("pe", [f"{tag}_gel{wnd}_{mc-2}"])
                        elif kc == 0 and wnd >= 1:
                            em.waits_only("pe", [f"{tag}_gel{wnd-1}_{mc+2}"])
                        elif kc == 0 and tt >= 1:
                            em.waits_only("pe", [f"{sfx}_f{tt-1}_gel7_{mc+2}"])
                        em.mm(PA[:, po:po + QC],
                              wtile(W1_O, half, "W1", kc, mc * 128, 128),
                              A(H1F_O + kc * QC, QC),
                              start=(kc == 0), stop=(kc == DC - 1),
                              waits=[wl, f"{tag}_h2"] if kc == 0 else (),
                              tok=f"{tag}_w1mm{mcol_g}" if kc == DC - 1
                                  else None)
                    em.op("act", lambda e, mcol_g=mcol_g, po=po: e.activation(
                        A(GG_O + mcol_g * QC, QC), PA[:, po:po + QC],
                        AF.Gelu, bias=b1c[:, mcol_g:mcol_g + 1]),
                        waits=[f"{tag}_w1mm{mcol_g}"],
                        tok=f"{tag}_gel{wnd}_{mc}")
            if tt == 0:
                dbg_store(f"{sfx}_g00", A(GG_O, 32 * QC), [f"{tag}_gel7_3"])
            for wnd in range(8):
                half = wnd % 2
                wl = wload(W2_O, half, "W2", wnd, stream="w2",
                           waits=[f"{tag}_xa{wnd-2}"] if wnd >= 2 else
                                 ([f"{sfx}_f{tt-1}_w2f"] if tt else
                                  [f"{sfx}_attf_{NQC-1}"]),
                           tag=f"{tag}_w2_{wnd}")
                po = (wnd % 2) * 512
                for kc in range(32):
                    em.mm(PB[:, po:po + QC],
                          wtile(W2_O, half, "W2", kc, 0, 128),
                          A(GG_O + kc * QC, QC),
                          start=(kc == 0), stop=(kc == 31),
                          waits=[wl, f"{tag}_gel7_3"] if kc == 0 else (),
                          tok=f"{tag}_w2mm{wnd}" if kc == 31 else None)
                em.op("dve", lambda e, wnd=wnd, po=po: e.scalar_tensor_tensor(
                    Rs(wnd, tt * QC, QC), PB[:, po:po + QC],
                    b2c[:, wnd:wnd + 1], Rs(wnd, tt * QC, QC),
                    ALU.add, ALU.add),
                    waits=[f"{tag}_w2mm{wnd}"], tok=f"{tag}_xa{wnd}")
            em.op("dve", lambda e: e.memset(stat[:, 7:8], 0.0),
                  waits=[f"{tag}_xa7"], tok=f"{tag}_w2f")
        dbg_store(f"{sfx}_xa", A(R_O, DC * T), [f"{sfx}_f{NQC-1}_w2f"])

        moecut = ("full" if "moe" in phases else
                  "p1" if "moe_p1" in phases else
                  "gate" if "moe_gate" in phases else
                  "top2" if "moe_top2" in phases else
                  "sct" if "moe_sct" in phases else
                  "tronly" if "moe_tronly" in phases else
                  "gmm" if "moe_gmm" in phases else
                  "hm" if "moe_hm" in phases else
                  "e16" if "moe_e16" in phases else None)
        if moecut is None:
            continue
        em.op("pool", lambda e: e.memset(A(E16_O, 2048), 1.0),
              waits=[f"{sfx}_wo_pf_{NQC-1}"], tok=f"{sfx}_E16m")
        for e_ in range(16):
            em.op("pool", lambda e, e_=e_: e.affine_select(
                ARENA[0:16, E16_O + e_ * 128:E16_O + (e_ + 1) * 128],
                ARENA[0:16, E16_O + e_ * 128:E16_O + (e_ + 1) * 128],
                [[0, 128]], ALU.is_equal, 0.0, base=-e_, channel_multiplier=1),
                waits=[f"{sfx}_E16m"],
                tok=f"{sfx}_E16" if e_ == 15 else None)
        st_toks = []
        if moecut == "e16":
            continue
        for tt in range(NQC):
            dbg_fence()
            tag = f"{sfx}_m{tt}"
            guard = ([f"{sfx}_f{NQC-1}_w2f"] if tt == 0 else
                     ([f"{sfx}_m{tt-1}_p2f"] if moecut == "full" else
                      ([f"{sfx}_m{tt-1}_z16_3"] if moecut == "p1" else
                       ([f"{sfx}_m{tt-1}_rtt"] if moecut == "gate" else
                        ([f"{sfx}_m{tt-1}_rt3"] if moecut == "top2" else
                         ([f"{sfx}_m{tt-1}_sct3"] if moecut == "sct" else
                          ([f"{sfx}_m{tt-1}_str3"] if moecut == "tronly" else
                           ([f"{sfx}_m{tt-1}_sig"] if moecut == "gmm" else
                           [f"{sfx}_m{tt-1}_hm"]))))))))
            gsq = guard + [f"{sfx}_f{NQC-1}_ss{DC-1}" if tt == 0 else
                           f"{sfx}_m{tt-1}_ss{DC-1}"]
            ln_rows(R_O, tt, guard, gsq, tag)
            for c in range(DC):
                dst = A(HM_O + c * QC, QC)
                em.op("dve", lambda e, dst=dst, c=c: e.tensor_tensor(
                    dst, Rs(c, tt * QC, QC), bc1[:, :], ALU.subtract),
                    waits=[f"{tag}_bc"])
                em.op("dve", lambda e, dst=dst: e.tensor_tensor(
                    dst, dst, bc2[:, :], ALU.mult),
                    tok=f"{tag}_hm" if c == DC - 1 else None)
            if tt == 0:
                dbg_store(f"{sfx}_hm0", A(HM_O, DC * QC), [f"{tag}_hm"])
            if moecut == "hm":
                continue
            for kc in range(DC):
                em.mm(PD[0:16, 0:QC], gwT[:, kc * 16:(kc + 1) * 16],
                      A(HM_O + kc * QC, QC),
                      start=(kc == 0), stop=(kc == DC - 1),
                      waits=[f"{tag}_hm", "gwT"] if kc == 0 else (),
                      tok=f"{tag}_gmm" if kc == DC - 1 else None)
            em.op("act", lambda e: e.activation(SCT[0:16, :], PD[0:16, 0:QC],
                                                AF.Sigmoid),
                  waits=[f"{tag}_gmm"], tok=f"{tag}_sig")
            if tt == 0:
                dbg_store(f"{sfx}_scores0", SCT[0:16, :], [f"{tag}_sig"])
            if moecut == "gmm":
                continue
            for i4 in range(4):
                em.tr(PC[:, 256 + i4 * 16:256 + (i4 + 1) * 16],
                      SCT[0:16, i4 * 128:(i4 + 1) * 128], identf[0:16, 0:16],
                      waits=[f"{tag}_sig"], tok=f"{tag}_str{i4}")
            if moecut == "tronly":
                continue
            for i4 in range(4):
                em.op("dve", lambda e, i4=i4: e.tensor_copy(
                    SCt[:, i4 * 16:(i4 + 1) * 16],
                    PC[:, 256 + i4 * 16:256 + (i4 + 1) * 16]),
                    waits=[f"{tag}_str3"],
                    tok=f"{tag}_sct{i4}")
            if moecut in ("sct", "tronly"):
                continue
            for i4 in range(4):
                sc = SCt[:, i4 * 16:(i4 + 1) * 16]
                em.op("dve", lambda e, sc=sc: e.reduce_max(stat[:, 0:1], sc, AX.X),
                      waits=[f"{tag}_sct{i4}"])
                em.op("dve", lambda e, sc=sc: e.tensor_scalar(
                    sgf[:, 0:16], sc, stat[:, 0:1], None, ALU.is_ge))
                em.op("dve", lambda e: e.reduce_sum(stat[:, 1:2], sgf[:, 0:16],
                                                    AX.X))
                em.op("dve", lambda e, sc=sc: e.scalar_tensor_tensor(
                    z1f[:, 0:16], sgf[:, 0:16], -1e30, sc, ALU.mult, ALU.add))
                em.op("dve", lambda e: e.reduce_max(stat[:, 2:3], z1f[:, 0:16],
                                                    AX.X))
                em.op("dve", lambda e: e.tensor_scalar(stat[:, 3:4], stat[:, 1:2],
                                                       2.0, None, ALU.is_ge))
                em.op("dve", lambda e: e.tensor_sub(stat[:, 4:5], stat[:, 0:1],
                                                    stat[:, 2:3]))
                em.op("dve", lambda e: e.scalar_tensor_tensor(
                    stat[:, 2:3], stat[:, 3:4], stat[:, 4:5], stat[:, 2:3],
                    ALU.mult, ALU.add))
                em.op("dve", lambda e: e.tensor_add(stat[:, 5:6], stat[:, 0:1],
                                                    stat[:, 2:3]))
                em.op("dve", lambda e: e.reciprocal(stat[:, 6:7], stat[:, 5:6]))
                em.op("dve", lambda e: e.tensor_scalar(stat[:, 6:7], stat[:, 6:7],
                                                       2.5, None, ALU.mult))
                em.op("dve", lambda e, sc=sc: e.tensor_scalar(
                    sgf[:, 0:16], sc, stat[:, 2:3], None, ALU.is_ge))
                em.op("dve", lambda e, sc=sc: e.tensor_mul(sgf[:, 0:16],
                                                           sgf[:, 0:16], sc))
                em.op("dve", lambda e: e.tensor_scalar(sgf[:, 0:16], sgf[:, 0:16],
                                                       stat[:, 6:7], None,
                                                       ALU.mult),
                      tok=f"{tag}_rt{i4}")
                if moecut == "top2":
                    continue
                em.tr(PD[0:16, i4 * 128:(i4 + 1) * 128],
                      sgf[:, 0:16], identf[:, :],
                      waits=[f"{tag}_rt{i4}"], tok=f"{tag}_rtr{i4}")
                em.op("dve", lambda e, i4=i4: e.tensor_copy(
                    RTT[0:16, i4 * 128:(i4 + 1) * 128],
                    PD[0:16, i4 * 128:(i4 + 1) * 128]),
                    waits=[f"{tag}_rtr{i4}"],
                    tok=f"{tag}_rtt" if i4 == 3 else None)
            if tt == 0:
                dbg_store(f"{sfx}_routeT0", RTT[0:16, :], [f"{tag}_rtt"])

            if moecut == "gate":
                continue
            for e_ in range(17):
                shared = (e_ == 16)
                wgn = "ws_gate" if shared else f"wg{e_}"
                wun = "ws_up" if shared else f"wu{e_}"
                half = e_ % 2
                base_w = ([f"{tag}_z{e_-2}_3"] if e_ >= 2 else
                          ([f"{sfx}_m{tt-1}_p2f" if moecut == "full" else
                            f"{sfx}_m{tt-1}_z16_3"] if tt else
                           [f"{sfx}_f{NQC-1}_w2f"]))
                wlg = wload(WG_O, half, wgn, 0, waits=base_w,
                            tag=f"{tag}_wg{e_}", stream="wg")
                wlu = wload(WU_O, half, wun, 0, waits=base_w,
                            tag=f"{tag}_wu{e_}", stream="wu")
                if not shared:
                    rb_psum = PEb[:, 0:QC] if e_ % 2 == 0 else PC[:, 512:512 + QC]
                    em.mm(rb_psum, ARENA[0:16, E16_O + e_ * 128:
                                         E16_O + (e_ + 1) * 128],
                          RTT[0:16, :],
                          waits=[f"{tag}_rtt", f"{sfx}_E16"] +
                                ([f"{tag}_rbf{e_-2}"] if e_ >= 2 else []),
                          tok=f"{tag}_rb{e_}")
                for m_ in range(4):
                    po = (m_ % 2) * 512
                    gwl = [wlg, f"{tag}_hm"]
                    uwl = [wlu]
                    if m_ < 2 and e_ >= 1:
                        gwl.append(f"{tag}_sl{e_-1}_{m_+2}")
                        uwl.append(f"{tag}_z{e_-1}_{m_+2}")
                    for kc in range(DC):
                        if kc == 0 and m_ >= 2:
                            em.waits_only("pe", [f"{tag}_z{e_}_{m_-2}"])
                        em.mm(PA[:, po:po + QC],
                              wtile(WG_O, half, wgn, kc, m_ * 128, 128),
                              A(HM_O + kc * QC, QC),
                              start=(kc == 0), stop=(kc == DC - 1),
                              waits=gwl if kc == 0 else (),
                              tok=f"{tag}_gp{e_}_{m_}" if kc == DC - 1 else None)
                    for kc in range(DC):
                        em.mm(PB[:, po:po + QC],
                              wtile(WU_O, half, wun, kc, m_ * 128, 128),
                              A(HM_O + kc * QC, QC),
                              start=(kc == 0), stop=(kc == DC - 1),
                              waits=uwl if kc == 0 else (),
                              tok=f"{tag}_up{e_}_{m_}" if kc == DC - 1 else None)
                    em.op("act", lambda e, po=po: e.activation(
                        sgf[:, :], PA[:, po:po + QC], AF.Silu),
                        waits=[f"{tag}_gp{e_}_{m_}"] +
                              ([f"{tag}_z{e_}_{m_-1}"] if m_ else
                               ([f"{tag}_z{e_-1}_3"] if e_ else
                                [f"{tag}_rtt"])),
                        tok=f"{tag}_sl{e_}_{m_}")
                    zdst = A(ZZ_O + (e_ * 4 + m_) * QC, QC)
                    if shared:
                        em.op("dve", lambda e, zdst=zdst, po=po: e.tensor_tensor(
                            zdst, sgf[:, :], PB[:, po:po + QC], ALU.mult),
                            waits=[f"{tag}_sl{e_}_{m_}", f"{tag}_up{e_}_{m_}"],
                            tok=f"{tag}_z{e_}_{m_}")
                    else:
                        em.op("dve", lambda e, po=po: e.tensor_tensor(
                            z1f[:, :], sgf[:, :], PB[:, po:po + QC], ALU.mult),
                            waits=[f"{tag}_sl{e_}_{m_}", f"{tag}_up{e_}_{m_}"])
                        rb_psum = (PEb[:, 0:QC] if e_ % 2 == 0
                                   else PC[:, 512:512 + QC])
                        em.op("dve", lambda e, zdst=zdst, rb_psum=rb_psum:
                              e.tensor_tensor(zdst, z1f[:, :], rb_psum,
                                              ALU.mult),
                              waits=[f"{tag}_rb{e_}"],
                              tok=f"{tag}_z{e_}_{m_}")
                if not shared:
                    em.op("dve", lambda e: e.tensor_copy(stat[:, 7:8],
                                                         stat[:, 7:8]),
                          waits=[f"{tag}_z{e_}_3"], tok=f"{tag}_rbf{e_}")
            if tt == 0:
                dbg_store(f"{sfx}_z00", A(ZZ_O, 4 * QC), [f"{tag}_z0_3"])

            if moecut == "p1":
                continue
            banks = [PA[:, 0:512], PA[:, 512:1024], PB[:, 0:512],
                     PB[:, 512:1024], PC[:, 0:512], PC[:, 512:1024],
                     PD[:, 0:512], PEb[:, 0:512]]
            for e_ in range(17):
                wdn = "ws_down" if e_ == 16 else f"wd{e_}"
                wl = wload(WDM_O, e_ % 2, wdn, 0, stream="wd",
                           waits=[f"{tag}_d{e_-2}"] if e_ >= 2 else
                                 [f"{tag}_z16_3"],
                           tag=f"{tag}_wd{e_}")
                for mc in range(DC):
                    for kc in range(4):
                        em.mm(banks[mc],
                              wtile(WDM_O, e_ % 2, wdn, kc, mc * 128, 128),
                              A(ZZ_O + (e_ * 4 + kc) * QC, QC),
                              start=(e_ == 0 and kc == 0),
                              stop=(e_ == 16 and kc == 3),
                              waits=[wl, f"{tag}_z{e_}_3"] if (kc == 0 and
                                                              mc == 0) else (),
                              tok=f"{tag}_d{e_}" if (mc == DC - 1 and kc == 3)
                                  else None)
            dbg_fence()
            OUTF = Af(OF_O, 4096)
            OUTT = Af(OU_O, 2048)
            for mc in range(DC):
                em.op("dve", lambda e, mc=mc: e.tensor_tensor(
                    OUTF[:, mc * 512:(mc + 1) * 512], Rs(mc, tt * QC, QC),
                    banks[mc], ALU.add),
                    waits=[f"{tag}_d16"], tok=f"{tag}_oev{mc}")
            for i4 in range(4):
                for mc in range(DC):
                    em.tr(PA[:, mc * 128:(mc + 1) * 128],
                          OUTF[:, mc * 512 + i4 * 128: mc * 512 + i4 * 128 + 128],
                          identf[:, :],
                          waits=([f"{tag}_oev{mc}"] if i4 == 0 else
                                 [f"{tag}_oc{i4-1}_{DC-1}"]),
                          tok=f"{tag}_otr{i4}_{mc}")
                for mc in range(DC):
                    ocw = [f"{tag}_otr{i4}_{DC-1}"]
                    if i4 >= 2 and mc == 0:
                        ocw.append(f"{tag}_st{i4-2}")
                    em.op("dve", lambda e, i4=i4, mc=mc: e.tensor_copy(
                        OUTT[:, (i4 % 2) * D + mc * 128:
                             (i4 % 2) * D + (mc + 1) * 128],
                        PA[:, mc * 128:(mc + 1) * 128]),
                        waits=ocw,
                        tok=f"{tag}_oc{i4}_{mc}")
                em.dma("act",
                       out_h[s * T + tt * QC + i4 * 128:
                             s * T + tt * QC + (i4 + 1) * 128, :],
                       OUTT[:, (i4 % 2) * D:(i4 % 2) * D + D],
                       waits=[f"{tag}_oc{i4}_{mc_}" for mc_ in range(DC)],
                       tok=f"{tag}_st{i4}", sem=f"st{i4 % 2}")
                st_toks.append(f"{tag}_st{i4}")
            em.op("dve", lambda e: e.memset(stat[:, 7:8], 0.0),
                  waits=[f"{tag}_st2", f"{tag}_st3", f"{tag}_oc3_7"],
                  tok=f"{tag}_p2f")
        if moecut == "full":
            prev_barrier = [f"{sfx}_m{NQC-1}_p2f"]
        else:
            prev_barrier = [(q, em.cnt[q]) for q in ("dve", "act", "pe", "pool")
                            if em.cnt[q] > 0]

    dmasems = [s for s in em.semnames
               if s not in ("pe", "dve", "act", "pool")]
    fin = [(s, em.cnt[s]) for s in dmasems if em.cnt[s] > 0]
    fin += [("dve", em.cnt["dve"]), ("act", em.cnt["act"]),
            ("pool", em.cnt["pool"])]
    em.waits_only("sync", fin)
    em.waits_only("act", fin)




def kernel(**inputs):
    x = np.ascontiguousarray(inputs["L0_attn_x"])
    nc = build_nc()
    base = {}
    for k, v in inputs.items():
        if k == "L0_attn_x":
            continue
        name = k.replace("L0_attn_", "").replace("L0_ffn_", "")
        base[name] = np.ascontiguousarray(v)
    in_maps = []
    for c in range(NCORES):
        m = dict(base)
        m["x"] = x[c * NSEQ:(c + 1) * NSEQ].reshape(NSEQ * T, D)
        in_maps.append(m)
    res = run_bass_kernel_spmd(nc, in_maps, core_ids=list(range(NCORES)))
    return np.concatenate(
        [np.asarray(r["out"]).reshape(NSEQ, T, D) for r in res.results], axis=0
    ).reshape(B, T, D)


# version 10 (final)
# speedup vs baseline: 1.0483x; 1.0483x over previous
import math
import numpy as np
from contextlib import ExitStack

import concourse.bass as bass
import concourse.mybir as mybir
from concourse.bass_utils import run_bass_kernel_spmd

F32 = mybir.dt.float32
BF16 = mybir.dt.bfloat16
I32 = mybir.dt.int32
AF = mybir.ActivationFunctionType
ALU = mybir.AluOpType
AX = mybir.AxisListType

NCORES = 8
B, T, D = 16, 2048, 1024
NSEQ = B // NCORES
H, DH = 16, 64
DC = D // 128
DFF = 4096
NE, M = 16, 512
QC = 512
NQC = T // QC
NKT = T // 128
EPS = 1e-5
LN1E4 = math.log(10000.0)
TWO_PI = 2.0 * math.pi
SHUF = [i + 1 - 2 * (i % 2) for i in range(32)]

AR_N = 84224
R_O = 0
HH_O = 16384
XT_O = 32768
HT_O = 40960
VS_O = 32768
TA_O = 34816
TB_O = 43008
KR_O = 32768
VT_O = 49152
QQ_O = 65792
OTH_O = 69888
EA_O = 73984
EB_O = 75008
WQ_O = 80128
WO_O = 82176
KW_O = 76032
GG_O = 32768
H1F_O = 49152
W1_O = 53248
W2_O = 61440
ZZ_O = 16384
HM_O = 51200
WG_O = 55296
WU_O = 63488
WDM_O = 71680
E16_O = 79872
OF_O = 16384
OU_O = 24576


class Emitter:
    def __init__(self):
        self.tok = {}
        self.semnames = (["syncq", "poolq", "actq", "pe", "dve", "act", "pool",
                          "vsq", "dbgq", "xq0", "xq1", "st0", "st1"] +
                         [f"ws_{s}{h}" for s in ("kw", "wq", "wo", "w1", "w2",
                                                 "wg", "wu", "wd")
                          for h in (0, 1)] +
                         [f"pq_{n}" for n in ("Wqkv", "Wo", "W1", "W2",
                                              "ws_gate", "ws_up", "ws_down")] +
                         [f"pq_e{e}" for e in range(16)])

    def begin(self, cur, engines, sems):
        self.cur = cur
        self.eng = engines
        self.sems = sems
        self.cnt = {k: 0 for k in self.semnames}
        self.lastwait = {}
        self.dry = cur is None

    def _resolve(self, w):
        if isinstance(w, tuple):
            return w
        assert w in self.tok, f"wait on undefined token {w}"
        return self.tok[w]

    def op(self, engine, fn, waits=(), inc=None, n=1, tok=None, isdma=False):
        chain = (engine in ("dve", "act", "pool")) and not isdma
        esem = {"sync": "syncq", "pool": "pool", "pe": "pe",
                "dve": "dve", "act": "act"}[engine]
        if inc is None and (tok is not None or chain):
            inc = esem
        allwaits = list(waits)
        if chain and self.cnt[esem] > 0:
            allwaits.append((esem, self.cnt[esem]))
        if self.cur == engine:
            h = self.eng
            byname = {}
            for w in allwaits:
                sem, val = self._resolve(w)
                byname[sem] = max(byname.get(sem, -1), val)
            for sem, val in byname.items():
                if self.lastwait.get(sem, -1) < val:
                    h.wait_ge(self.sems[sem], val)
                    self.lastwait[sem] = val
            instr = fn(h)
            if inc is not None:
                instr.then_inc(self.sems[inc], n)
        elif not self.dry:
            for w in allwaits:
                self._resolve(w)
        if inc is not None:
            self.cnt[inc] += n
            if tok is not None:
                self.tok[tok] = (inc, self.cnt[inc])

    def waits_only(self, engine, waits):
        if self.cur == engine:
            h = self.eng
            byname = {}
            for w in waits:
                sem, val = self._resolve(w)
                byname[sem] = max(byname.get(sem, -1), val)
            for sem, val in byname.items():
                if self.lastwait.get(sem, -1) < val:
                    h.wait_ge(self.sems[sem], val)
                    self.lastwait[sem] = val
        elif not self.dry:
            for w in waits:
                self._resolve(w)

    def dma(self, queue, out, in_, waits=(), tok=None, sem=None):
        semname = sem or {"sync": "syncq", "pool": "poolq", "act": "actq"}[queue]
        self.op(queue, lambda e: e.dma_start(out=out, in_=in_),
                waits, inc=semname, n=16, tok=tok, isdma=True)

    def mm(self, out, lhsT, rhs, start=True, stop=True, waits=(), tok=None):
        self.op("pe", lambda e: e.matmul(out, lhsT, rhs, start=start, stop=stop),
                waits, inc="pe" if tok else None, n=1, tok=tok)

    def tr(self, out, in_, ident, waits=(), tok=None):
        self.op("pe", lambda e: e.transpose(out, in_, ident),
                waits, inc="pe" if tok else None, n=1, tok=tok)


def build_nc(dbg=None, nseq=NSEQ, phases=('load','kv','attn','ffn','moe')):
    dbg = dbg or set()
    nc = bass.Bass("TRN2", target_bir_lowering=False)
    em = Emitter()

    x_h = nc.dram_tensor("x", [nseq * T, D], F32, kind="ExternalInput").ap()
    out_h = nc.dram_tensor("out", [nseq * T, D], F32, kind="ExternalOutput").ap()
    wf = {}
    for name, shape in [
        ("Wqkv", [D, 3 * D]), ("Wo", [D, D]), ("W1", [D, DFF]), ("W2", [DFF, D]),
        ("ln1_g", [D]), ("ln1_b", [D]), ("ln2_g", [D]), ("ln2_b", [D]),
        ("b1", [DFF]), ("b2", [D]), ("gate_w", [NE, D]),
        ("w_gate", [NE, D, M]), ("w_up", [NE, D, M]), ("w_down", [NE, M, D]),
        ("ws_gate", [D, M]), ("ws_up", [D, M]), ("ws_down", [M, D]),
    ]:
        wf[name] = nc.dram_tensor(name, shape, F32, kind="ExternalInput").ap()

    scr = {}
    scr_meta = {}

    def mk_scr(name, K, Mtot, mw):
        KC = K // 128
        nmg = Mtot // mw
        scr[name] = nc.dram_tensor("scr_" + name, [nmg, 128, KC * mw], BF16,
                                   kind="Internal").ap()
        scr_meta[name] = (KC, Mtot, mw, nmg)

    mk_scr("Wqkv", D, 3 * D, 256)
    mk_scr("Wo", D, D, 128)
    mk_scr("W1", D, DFF, 512)
    mk_scr("W2", DFF, D, 128)
    mk_scr("ws_gate", D, M, 512)
    mk_scr("ws_up", D, M, 512)
    mk_scr("ws_down", M, D, 1024)
    for e in range(NE):
        mk_scr(f"wg{e}", D, M, 512)
        mk_scr(f"wu{e}", D, M, 512)
        mk_scr(f"wd{e}", M, D, 1024)

    dbg_outs = {}
    def mkdbg(name, shape, dtype=F32):
        if name in dbg:
            dbg_outs[name] = nc.dram_tensor("dbg_" + name, shape, dtype,
                                            kind="ExternalOutput").ap()
    for s in range(nseq):
        for nm in ("h0", "R0", "kr", "x1", "xa"):
            mkdbg(f"s{s}_{nm}", [128, DC * T])
        mkdbg(f"s{s}_vt", [128, NKT * H * 65])
        mkdbg(f"s{s}_q0", [128, DC * QC])
        mkdbg(f"s{s}_ot0", [128, DC * QC])
        mkdbg(f"s{s}_hm0", [128, DC * QC])
        mkdbg(f"s{s}_scores0", [16, QC])
        mkdbg(f"s{s}_routeT0", [16, QC])
        mkdbg(f"s{s}_cneg", [128, 16])
        mkdbg(f"s{s}_z00", [128, 4 * QC])
        mkdbg(f"s{s}_g00", [128, 32 * QC])

    with ExitStack() as ctx:
        def sb(name, shape, dtype):
            return ctx.enter_context(nc.sbuf_tensor(name, shape, dtype))

        ARENA = sb("ARENA", [128, AR_N], BF16)
        identf = sb("identf", [128, 128], F32)
        ones1b = sb("ones1b", [1, 128], BF16)
        onescb = sb("onescb", [128, 1], BF16)
        COS = sb("COS", [128, T], BF16)
        SINPM = sb("SINPM", [128, T], BF16)
        maskB = sb("maskB", [128, 896], BF16)
        ln1g = sb("ln1g", [128, DC], F32)
        ln1b_ = sb("ln1b", [128, DC], F32)
        ln2g = sb("ln2g", [128, DC], F32)
        ln2b_ = sb("ln2b", [128, DC], F32)
        b2c = sb("b2c", [128, DC], F32)
        b1c = sb("b1c", [128, DFF // 128], F32)
        gwT = sb("gwT", [128, DC * 16], BF16)
        consts = sb("consts", [128, 8], F32)
        cnegc = sb("cnegc", [128, 16], F32)
        pcol = sb("pcol", [128, 4], F32)
        rowb = sb("rowb", [1, 5248], BF16)
        rowf = sb("rowf", [1, 1664], F32)
        bc1 = sb("bc1", [128, QC], BF16)
        bc2 = sb("bc2", [128, QC], BF16)
        sgf = sb("sgf", [128, QC], F32)
        z1f = sb("z1f", [128, QC], F32)
        sqb = sb("sqb", [128, QC], BF16)
        stat = sb("stat", [128, 16], F32)
        lrow = sb("lrow", [1, 1024], BF16)
        cnegP = sb("cnegP", [128, 8], F32)
        rpA = sb("rpA", [128, QC], BF16)
        rpB = sb("rpB", [128, QC], BF16)
        knb = sb("knb", [128, QC], BF16)

        PS4 = ctx.enter_context(nc.psum_tensor("PS4", [128, 2048], F32))
        PA = PS4[:, 0:1024]
        PB = PS4[:, 1024:2048]
        PC = ctx.enter_context(nc.psum_tensor("PC", [128, 1024], F32))
        PD = ctx.enter_context(nc.psum_tensor("PD", [128, 512], F32))
        PEb = ctx.enter_context(nc.psum_tensor("PEb", [128, 512], F32))

        sems = {}
        for s_ in em.semnames:
            sems[s_] = ctx.enter_context(nc.semaphore(s_))

        S = dict(locals())

        def sched():
            build_schedule(em, S, nseq, dbg_outs, phases)

        block = ctx.enter_context(nc.Block())
        em.begin(None, None, sems)
        sched()
        for key, attr in (("sync", "sync"), ("pool", "gpsimd"), ("pe", "tensor"),
                          ("dve", "vector"), ("act", "scalar")):
            def body(e, key=key):
                em.begin(key, e, sems)
                sched()
            getattr(block, attr)(body)

    nc._dbg_outs = dbg_outs
    nc._out_ap = out_h
    return nc


def build_schedule(em, S, nseq, dbg_outs, phases):
    ARENA = S["ARENA"]
    x_h, out_h, wf, scr, scr_meta = (S["x_h"], S["out_h"], S["wf"], S["scr"],
                                     S["scr_meta"])
    identf, ones1b, onescb = S["identf"], S["ones1b"], S["onescb"]
    COS, SINPM, maskB = S["COS"], S["SINPM"], S["maskB"]
    ln1g, ln1b_, ln2g, ln2b_ = S["ln1g"], S["ln1b_"], S["ln2g"], S["ln2b_"]
    b2c, b1c, gwT, consts = S["b2c"], S["b1c"], S["gwT"], S["consts"]
    cnegc, pcol = S["cnegc"], S["pcol"]
    rowb, rowf = S["rowb"], S["rowf"]
    bc1, bc2, sgf, z1f, sqb = S["bc1"], S["bc2"], S["sgf"], S["z1f"], S["sqb"]
    stat, lrow = S["stat"], S["lrow"]
    PA, PB, PC, PD, PEb = S["PA"], S["PB"], S["PC"], S["PD"], S["PEb"]
    PS4, cnegP = S["PS4"], S["cnegP"]
    rpA, rpB, knb = S["rpA"], S["rpB"], S["knb"]

    def A(off, n):
        return ARENA[:, off:off + n]

    def Af(off, n):
        return ARENA[:, off:off + 2 * n].bitcast(F32)

    SCT = Af(81920, 512)[0:16, :]
    RTT = A(82944, 512)[0:16, :]
    SCt = Af(83456, 64)

    dbg_on = lambda nm: nm in dbg_outs

    def dbg_store(nm, src_ap, waits):
        if dbg_on(nm):
            em.dma("pool", dbg_outs[nm], src_ap, waits=waits, sem="dbgq")

    def dbg_fence():
        if dbg_outs:
            em.op("dve", lambda e: e.memset(stat[:, 7:8], 0.0),
                  waits=[("dbgq", em.cnt["dbgq"])])
            em.op("act", lambda e: e.activation(stat[:, 7:8], stat[:, 7:8],
                                                AF.Copy),
                  waits=[("dbgq", em.cnt["dbgq"]), ("dve", em.cnt["dve"])])

    em.op("pool", lambda e: e.memset(consts[:, 1:8], 0.0))
    em.op("pool", lambda e: e.memset(consts[:, 0:1], EPS))
    em.op("pool", lambda e: e.memset(ones1b[:, :], 1.0))
    em.op("pool", lambda e: e.memset(onescb[:, :], 1.0))
    em.op("pool", lambda e: e.memset(Af(TA_O, 128)[:, 0:128], 1.0),
          tok="po_ones")
    em.op("pool", lambda e: e.affine_select(identf[:, :], Af(TA_O, 128)[:, 0:128],
                                            [[1, 128]], ALU.is_equal, 0.0,
                                            base=0, channel_multiplier=-1),
          waits=["po_ones"], tok="po_ident")
    em.op("pool", lambda e: e.memset(maskB[:, :], 1.0), tok="po_mb")
    em.op("pool", lambda e: e.affine_select(maskB[:, :], maskB[:, :],
                                            [[1, 896]], ALU.is_ge, 0.0,
                                            base=-384, channel_multiplier=-1),
          waits=["po_mb"], tok="po_mask")
    em.op("pool", lambda e: e.iota(Af(TA_O, T)[:, :], [[1, T]], base=0,
                                   channel_multiplier=0,
                                   allow_small_or_imprecise_dtypes=True),
          waits=["po_ident"], tok="po_iota")
    em.op("pool", lambda e: e.iota(pcol[:, 0:1], [[0, 1]], base=0,
                                   channel_multiplier=1,
                                   allow_small_or_imprecise_dtypes=True),
          tok="iota_done")
    em.op("dve", lambda e: e.tensor_scalar(pcol[:, 1:2], pcol[:, 0:1], 1.0 / 64,
                                           None, ALU.mult),
          waits=["iota_done", "po_iota"])
    em.op("dve", lambda e: e.tensor_copy(pcol[:, 2:3].bitcast(I32), pcol[:, 1:2]))
    em.op("dve", lambda e: e.tensor_copy(pcol[:, 1:2], pcol[:, 2:3].bitcast(I32)))
    em.op("dve", lambda e: e.tensor_scalar(pcol[:, 2:3], pcol[:, 1:2], 64.0, None,
                                           ALU.mult))
    em.op("dve", lambda e: e.tensor_scalar(pcol[:, 2:3], pcol[:, 2:3],
                                           pcol[:, 0:1], None, ALU.is_gt))
    em.op("dve", lambda e: e.tensor_sub(pcol[:, 1:2], pcol[:, 1:2], pcol[:, 2:3]))
    em.op("dve", lambda e: e.scalar_tensor_tensor(pcol[:, 1:2], pcol[:, 1:2],
                                                  -64.0, pcol[:, 0:1],
                                                  ALU.mult, ALU.add))
    em.op("dve", lambda e: e.tensor_scalar(pcol[:, 2:3], pcol[:, 1:2], 0.5, None,
                                           ALU.mult))
    em.op("dve", lambda e: e.tensor_copy(pcol[:, 3:4].bitcast(I32), pcol[:, 2:3]))
    em.op("dve", lambda e: e.tensor_copy(pcol[:, 2:3], pcol[:, 3:4].bitcast(I32)))
    em.op("dve", lambda e: e.tensor_scalar(pcol[:, 3:4], pcol[:, 2:3], 2.0, None,
                                           ALU.mult))
    em.op("dve", lambda e: e.tensor_scalar(pcol[:, 3:4], pcol[:, 3:4],
                                           pcol[:, 1:2], None, ALU.is_gt))
    em.op("dve", lambda e: e.tensor_sub(pcol[:, 2:3], pcol[:, 2:3], pcol[:, 3:4]))
    em.op("dve", lambda e: e.scalar_tensor_tensor(pcol[:, 3:4], pcol[:, 2:3], -2.0,
                                                  pcol[:, 1:2], ALU.mult, ALU.add))
    em.op("dve", lambda e: e.tensor_scalar(pcol[:, 3:4], pcol[:, 3:4], 2.0, -1.0,
                                           ALU.mult, ALU.add), tok="jcol")
    em.op("act", lambda e: e.activation(pcol[:, 1:2], pcol[:, 2:3], AF.Exp,
                                        scale=-LN1E4 / 32), waits=["jcol"],
          tok="invf")
    em.op("dve", lambda e: e.tensor_scalar(Af(TA_O, T)[:, :], Af(TA_O, T)[:, :],
                                           pcol[:, 1:2], None, ALU.mult),
          waits=["invf"])
    em.op("dve", lambda e: e.tensor_scalar(Af(TB_O, T)[:, :], Af(TA_O, T)[:, :],
                                           1.0 / TWO_PI, None, ALU.mult))
    em.op("dve", lambda e: e.tensor_copy(Af(TB_O, T)[:, :].bitcast(I32),
                                         Af(TB_O, T)[:, :]))
    em.op("dve", lambda e: e.tensor_copy(Af(TB_O, T)[:, :],
                                         Af(TB_O, T)[:, :].bitcast(I32)))
    em.op("dve", lambda e: e.scalar_tensor_tensor(Af(TA_O, T)[:, :],
                                                  Af(TB_O, T)[:, :], -TWO_PI,
                                                  Af(TA_O, T)[:, :],
                                                  ALU.mult, ALU.add))
    em.op("dve", lambda e: e.tensor_scalar(Af(TB_O, T)[:, :], Af(TA_O, T)[:, :],
                                           math.pi, None, ALU.is_gt))
    em.op("dve", lambda e: e.scalar_tensor_tensor(Af(TA_O, T)[:, :],
                                                  Af(TB_O, T)[:, :], -TWO_PI,
                                                  Af(TA_O, T)[:, :],
                                                  ALU.mult, ALU.add),
          tok="mready")
    em.op("act", lambda e: e.activation(Af(TB_O, T)[:, :], Af(TA_O, T)[:, :],
                                        AF.Sin), waits=["mready"], tok="sinm")
    em.op("act", lambda e: e.activation(Af(TA_O, T)[:, :], Af(TA_O, T)[:, :],
                                        AF.Sin, scale=0.5), tok="sinh")
    em.op("dve", lambda e: e.tensor_scalar(SINPM[:, :], Af(TB_O, T)[:, :],
                                           pcol[:, 3:4], None, ALU.mult),
          waits=["sinm"])
    em.op("dve", lambda e: e.tensor_mul(Af(TA_O, T)[:, :], Af(TA_O, T)[:, :],
                                        Af(TA_O, T)[:, :]), waits=["sinh"])
    em.op("dve", lambda e: e.tensor_scalar(COS[:, :], Af(TA_O, T)[:, :], -2.0, 1.0,
                                           ALU.mult, ALU.add), tok="trig")

    VS = Af(VS_O, 1024)
    em.dma("sync", VS[0:8, 0:128],
           wf["ln1_g"].rearrange("(a b) -> a b", b=128), waits=["trig"],
           tok="ld_ln1g", sem="vsq")
    em.dma("sync", VS[8:16, 0:128], wf["ln1_b"].rearrange("(a b) -> a b", b=128),
           tok="ld_ln1b", sem="vsq")
    em.dma("sync", VS[16:24, 0:128], wf["ln2_g"].rearrange("(a b) -> a b", b=128),
           tok="ld_ln2g", sem="vsq")
    em.dma("sync", VS[24:32, 0:128], wf["ln2_b"].rearrange("(a b) -> a b", b=128),
           tok="ld_ln2b", sem="vsq")
    em.dma("sync", VS[32:40, 0:128], wf["b2"].rearrange("(a b) -> a b", b=128),
           tok="ld_b2", sem="vsq")
    em.waits_only("pe", ["ld_ln1g", "ld_ln1b", "ld_ln2g", "ld_ln2b", "ld_b2",
                         "trig"])
    em.tr(PA[:, 0:40], VS[0:40, 0:128], identf[0:40, 0:40], tok="tr_lnv")
    for i, dst in enumerate((ln1g, ln1b_, ln2g, ln2b_, b2c)):
        em.op("dve", lambda e, dst=dst, i=i: e.tensor_copy(
            dst[:, :], PA[:, 8 * i:8 * i + 8]),
            waits=["tr_lnv"], tok=f"lncol{i}")
    em.dma("sync", VS[0:32, 0:128], wf["b1"].rearrange("(a b) -> a b", b=128),
           waits=["lncol4"], tok="ld_b1", sem="vsq")
    em.waits_only("pe", ["ld_b1", "lncol4"])
    em.tr(PA[:, 8:40], VS[0:32, 0:128], identf[0:32, 0:32], tok="tr_b1")
    em.op("dve", lambda e: e.tensor_copy(b1c[:, :], PA[:, 8:40]), waits=["tr_b1"],
          tok="b1ready")
    em.dma("sync", VS[0:16, 0:1024], wf["gate_w"], waits=["b1ready"],
           tok="ld_gw", sem="vsq")
    for kc in range(DC):
        em.tr(PB[:, kc * 16:(kc + 1) * 16], VS[0:16, kc * 128:(kc + 1) * 128],
              identf[0:16, 0:16],
              waits=["ld_gw"] if kc == 0 else (), tok=f"tr_gw{kc}")
    em.op("dve", lambda e: e.tensor_copy(gwT[:, :], PB[:, 0:128]),
          waits=[f"tr_gw{DC-1}"], tok="gwT")

    pc_tok = {}
    for name in scr:
        src = {"Wqkv": wf["Wqkv"], "Wo": wf["Wo"], "W1": wf["W1"],
               "W2": wf["W2"], "ws_gate": wf["ws_gate"], "ws_up": wf["ws_up"],
               "ws_down": wf["ws_down"]}.get(name)
        if src is None:
            e_ = int(name[2:])
            src = {"wg": wf["w_gate"], "wu": wf["w_up"],
                   "wd": wf["w_down"]}[name[:2]] [e_]
        KC, Mtot, mw, nmg = scr_meta[name]
        gsem = f"pq_e{name[2:]}" if name[:2] in ("wg", "wu", "wd") else f"pq_{name}"
        for mg in range(nmg):
            em.dma("pool",
                   scr[name][mg].rearrange("p (kc m) -> p kc m", m=mw),
                   src[:, mg * mw:(mg + 1) * mw].rearrange(
                       "(kc p) m -> p kc m", p=128),
                   tok=f"pc_{name}_{mg}" if mg == nmg - 1 else None, sem=gsem)
        pc_tok[name] = f"pc_{name}_{nmg-1}"
    for e_ in range(NE):
        pc_tok[f"wg{e_}"] = pc_tok[f"wd{e_}"]
        pc_tok[f"wu{e_}"] = pc_tok[f"wd{e_}"]

    def wload(off, half, name, mg, waits=(), tag="", stream="kw"):
        KC, Mtot, mw, nmg = scr_meta[name]
        w_cols = KC * mw
        dst = A(off + half * w_cols, w_cols)
        em.dma("sync", dst, scr[name][mg], waits=list(waits) + [pc_tok[name]],
               tok="w_" + tag, sem=f"ws_{stream}{half}")
        return "w_" + tag

    def wtile(off, half, name, kc, mcol, w):
        KC, Mtot, mw, nmg = scr_meta[name]
        w_cols = KC * mw
        base = off + half * w_cols + kc * mw + mcol
        return ARENA[:, base:base + w]

    prev_barrier = []

    for s in range(nseq):
        sfx = f"s{s}"
        HHs = lambda c, t0, n: A(HH_O + c * T + t0, n)
        Rs = lambda c, t0, n: A(R_O + c * T + t0, n)
        KRs = lambda c, t0, n: A(KR_O + c * T + t0, n)

        def ld_slices(ct):
            p = ct % 2
            xdst = Af(XT_O, 4 * D)[:, (ct % 4) * D:(ct % 4 + 1) * D]
            htok = Af(HT_O + p * 2048, D)
            ss = [stat[:, 8 * p + k:8 * p + k + 1] for k in range(6)]
            return p, xdst, htok, ss

        def ld_stage0(ct):
            _, xdst, _, _ = ld_slices(ct)
            wl = []
            if ct == 0:
                wl = list(prev_barrier) + (["gwT", "trig"] if s == 0 else [])
            elif ct in (1, 2) and s == 0:
                wl = ["trig"]
            if ct >= 4:
                wl.append(f"{sfx}_ld{ct-4}_done")
            em.dma("sync", xdst, x_h[s * T + ct * 128: s * T + (ct + 1) * 128, :],
                   waits=wl, tok=f"{sfx}_x{ct}", sem=f"xq{ct % 2}")

        def ld_stage1(ct):
            p, xdst, htok, (s0, s1, s2, s3, s4, s5) = ld_slices(ct)
            em.op("dve", lambda e, xdst=xdst, s0=s0: e.reduce_sum(s0, xdst, AX.X),
                  waits=[f"{sfx}_x{ct}"] + (list(prev_barrier) if ct == 0 else []))
            em.op("dve", lambda e, s0=s0: e.tensor_scalar(s0, s0,
                                                          1.0 / D, None, ALU.mult),
                  tok=f"{sfx}_mu0_{ct}")
            sq_w = [f"{sfx}_x{ct}"]
            if ct >= 2:
                sq_w += [f"{sfx}_trh{ct-2}", f"{sfx}_var2_{ct-2}"]
            else:
                sq_w += ["trig"] + (list(prev_barrier) if s else [])
            em.op("act", lambda e, xdst=xdst, htok=htok, s1=s1: e.activation(
                htok[:, :], xdst, AF.Square, accum_out=s1),
                waits=sq_w, tok=f"{sfx}_sq0_{ct}")
            em.op("dve", lambda e, s0=s0, s2=s2: e.tensor_mul(s2, s0, s0),
                  waits=[f"{sfx}_mu0_{ct}", f"{sfx}_sq0_{ct}"])
            em.op("dve", lambda e, s1=s1, s2=s2: e.scalar_tensor_tensor(
                s2, s1, 1.0 / D, s2, ALU.mult,
                ALU.subtract), tok=f"{sfx}_var0_{ct}")
            em.op("act", lambda e, s2=s2: e.activation(s2, s2, AF.Ln,
                                                       bias=consts[:, 0:1]),
                  waits=[f"{sfx}_var0_{ct}"])
            em.op("act", lambda e, s2=s2: e.activation(s2, s2, AF.Exp,
                                                       scale=-0.5),
                  tok=f"{sfx}_r0_{ct}")
            em.op("dve", lambda e, xdst=xdst, htok=htok, s0=s0, s2=s2:
                  e.tensor_scalar(htok[:, :], xdst, s0, s2, ALU.subtract,
                                  ALU.mult),
                  waits=[f"{sfx}_r0_{ct}"], tok=f"{sfx}_h0t_{ct}")

        def ld_stage2(ct):
            p, xdst, htok, (s0, s1, s2, s3, s4, s5) = ld_slices(ct)
            em.op("dve", lambda e, htok=htok, s3=s3: e.reduce_sum(s3, htok[:, :],
                                                                  AX.X),
                  waits=[f"{sfx}_trmu{ct-2}"] if ct >= 2 else ())
            em.op("dve", lambda e, s3=s3: e.tensor_scalar(s3, s3,
                                                          1.0 / D, None, ALU.mult),
                  tok=f"{sfx}_mu2_{ct}")
            em.op("act", lambda e, htok=htok, s4=s4: e.activation(
                sgf[:, :], htok[:, 0:512], AF.Square, accum_out=s4),
                waits=[f"{sfx}_h0t_{ct}"] +
                      ([f"{sfx}_trr{ct-2}"] if ct >= 2 else []))
            em.op("act", lambda e, htok=htok, s5=s5: e.activation(
                z1f[:, :], htok[:, 512:1024], AF.Square, accum_out=s5),
                tok=f"{sfx}_sq2_{ct}")
            em.op("dve", lambda e, s4=s4, s5=s5: e.tensor_add(s4, s4, s5),
                  waits=[f"{sfx}_mu2_{ct}", f"{sfx}_sq2_{ct}"])
            em.op("dve", lambda e, s3=s3, s5=s5: e.tensor_mul(s5, s3, s3))
            em.op("dve", lambda e, s4=s4, s5=s5: e.scalar_tensor_tensor(
                s5, s4, 1.0 / D, s5, ALU.mult,
                ALU.subtract), tok=f"{sfx}_var2_{ct}")
            em.op("act", lambda e, s5=s5: e.activation(s5, s5, AF.Ln,
                                                       bias=consts[:, 0:1]),
                  waits=[f"{sfx}_var2_{ct}"])
            em.op("act", lambda e, s5=s5: e.activation(s5, s5, AF.Exp,
                                                       scale=-0.5),
                  tok=f"{sfx}_r2c_{ct}")
            em.waits_only("pe", [f"{sfx}_h0t_{ct}"])
            if ct == 0 and s == 0:
                em.waits_only("pe", ["gwT"])
            if ct >= 1:
                em.waits_only("pe", [f"{sfx}_ld{ct-1}_done", f"{sfx}_rows{ct-1}"])
            for c in range(DC):
                em.tr(PA[:, c * 128:(c + 1) * 128],
                      xdst[:, c * 128:(c + 1) * 128], identf[:, :],
                      tok=f"{sfx}_trx{ct}" if c == DC - 1 else None)
                em.tr(PB[:, c * 128:(c + 1) * 128],
                      htok[:, c * 128:(c + 1) * 128], identf[:, :],
                      tok=f"{sfx}_trh{ct}" if c == DC - 1 else None)
            PDx = PD if ct % 2 == 0 else PEb
            em.waits_only("pe", [f"{sfx}_r2c_{ct}"])
            em.tr(PDx[0:1, 0:128], s3, identf[:, :],
                  tok=f"{sfx}_trmu{ct}")
            em.tr(PDx[0:1, 128:256], s5, identf[:, :],
                  tok=f"{sfx}_trr{ct}")
            hhd = A(HH_O, DC * T).rearrange("p (c x) -> p c x", x=T)[
                :, :, ct * 128:(ct + 1) * 128]
            rrd = A(R_O, DC * T).rearrange("p (c x) -> p c x", x=T)[
                :, :, ct * 128:(ct + 1) * 128]
            em.op("dve", lambda e, hhd=hhd: e.tensor_copy(
                hhd, PB[:, :].rearrange("p (c x) -> p c x", x=128)),
                waits=[f"{sfx}_trh{ct}"])
            em.op("dve", lambda e, rrd=rrd, hhd=hhd: e.tensor_tensor(
                rrd, PA[:, :].rearrange("p (c x) -> p c x", x=128), hhd,
                ALU.add),
                waits=[f"{sfx}_trx{ct}"], tok=f"{sfx}_ld{ct}_done")
            em.op("dve", lambda e, PDx=PDx: e.tensor_copy(
                rowb[0:1, ct * 128:(ct + 1) * 128], PDx[0:1, 0:128]),
                waits=[f"{sfx}_trr{ct}"])
            em.op("dve", lambda e, PDx=PDx: e.tensor_copy(
                rowb[0:1, T + ct * 128: T + (ct + 1) * 128],
                PDx[0:1, 128:256]),
                waits=[f"{sfx}_trr{ct}"], tok=f"{sfx}_rows{ct}")

        for ct0 in range(min(3, NKT)):
            ld_stage0(ct0)
        ld_stage1(0)
        for ct in range(1, NKT):
            if ct + 2 < NKT:
                ld_stage0(ct + 2)
            ld_stage1(ct)
            ld_stage2(ct - 1)
        ld_stage2(NKT - 1)
        dbg_store(f"{sfx}_h0", A(HH_O, DC * T), [f"{sfx}_ld{NKT-1}_done"])
        dbg_store(f"{sfx}_R0", A(R_O, DC * T), [f"{sfx}_ld{NKT-1}_done"])

        def h1_chunk(h1off, tc, pc_guard, tag):
            em.mm(PC[:, 0:QC], ones1b[0:1, :], rowb[0:1, tc * QC:(tc + 1) * QC],
                  waits=list(pc_guard) + [f"{sfx}_rows{NKT-1}"],
                  tok=f"{sfx}_bmu_{tag}")
            em.mm(PC[:, 512:512 + QC], ones1b[0:1, :],
                  rowb[0:1, T + tc * QC: T + (tc + 1) * QC],
                  tok=f"{sfx}_br_{tag}")
            em.op("dve", lambda e: e.tensor_copy(bc1[:, :], PC[:, 0:QC]),
                  waits=[f"{sfx}_bmu_{tag}"])
            em.op("dve", lambda e: e.tensor_copy(bc2[:, :], PC[:, 512:512 + QC]),
                  waits=[f"{sfx}_br_{tag}"], tok=f"{sfx}_bc_{tag}")
            for c in range(DC):
                dst = A(h1off + c * QC, QC)
                em.op("dve", lambda e, dst=dst, c=c: e.tensor_tensor(
                    dst, HHs(c, tc * QC, QC), bc1[:, :], ALU.subtract),
                    waits=[f"{sfx}_bc_{tag}"])
                em.op("dve", lambda e, dst=dst: e.tensor_tensor(
                    dst, dst, bc2[:, :], ALU.mult))
                em.op("dve", lambda e, dst=dst, c=c: e.tensor_scalar(
                    dst, dst, ln1g[:, c:c + 1], ln1b_[:, c:c + 1],
                    ALU.mult, ALU.add),
                    tok=f"{sfx}_h1_{tag}" if c == DC - 1 else None)

        if "kv" not in phases:
            continue
        dbg_fence()
        vt4 = ARENA[:, VT_O:VT_O + NKT * H * 65].rearrange(
            "p (kt h w) -> p kt h w", h=H, w=65)
        em.op("pool", lambda e: e.memset(vt4[:, :, :, 64:65], 1.0),
              waits=list(prev_barrier), tok=f"{sfx}_vones")
        for step in range(NQC):
            tc = qc = step
            pcg = [f"{sfx}_wo_pf_{tc-1}"] if tc else []
            h1_chunk(OTH_O, tc, pcg, f"kv{tc}")
            for wnd in range(4):
                half = wnd % 2
                if wnd >= 2:
                    kww = [f"{sfx}_kmm{tc}_{(wnd-2)*2+1}"]
                elif tc:
                    kww = [f"{sfx}_vmm{tc-1}_{wnd+2}_3"]
                else:
                    kww = list(prev_barrier)
                wl = wload(KW_O, half, "Wqkv", 4 + wnd, waits=kww,
                           tag=f"{sfx}_k{tc}_{wnd}")
                for mc in range(2):
                    mcol_g = wnd * 2 + mc
                    pbank, po = (PA, (mcol_g % 2) * 512) if mcol_g < 4 else \
                                (PB, (mcol_g % 2) * 512)
                    for kc in range(DC):
                        if kc == 0 and mcol_g >= 2:
                            em.waits_only("pe", [f"{sfx}_kev{tc}_{mcol_g-2}"])
                        kmw = ()
                        if kc == 0:
                            kmw = [wl, f"{sfx}_h1_kv{tc}"]
                            if mcol_g == 0 and tc >= 1:
                                kmw.append(f"{sfx}_ex{tc-1}_7_{4*tc-1}")
                        em.mm(pbank[:, po:po + QC],
                              wtile(KW_O, half, "Wqkv", kc, mc * 128, 128),
                              A(OTH_O + kc * QC, QC),
                              start=(kc == 0), stop=(kc == DC - 1),
                              waits=kmw,
                              tok=f"{sfx}_kmm{tc}_{mcol_g}" if kc == DC - 1
                                  else None)
                    em.op("dve", lambda e, mcol_g=mcol_g, pbank=pbank, po=po:
                          e.tensor_copy(KRs(mcol_g, tc * QC, QC),
                                        pbank[:, po:po + QC]),
                          waits=[f"{sfx}_kmm{tc}_{mcol_g}"],
                          tok=f"{sfx}_kev{tc}_{mcol_g}")
                    ksl = KRs(mcol_g, tc * QC, QC)
                    csl = COS[:, tc * QC:(tc + 1) * QC]
                    ssl = SINPM[:, tc * QC:(tc + 1) * QC]
                    em.op("dve", lambda e, ksl=ksl: e.stream_shuffle(
                        sqb[:, :], ksl, SHUF),
                        waits=[f"{sfx}_kev{tc}_{mcol_g}", "trig"])
                    em.op("dve", lambda e, ksl=ksl, csl=csl: e.tensor_tensor(
                        rpA[:, :], ksl, csl, ALU.mult))
                    em.op("dve", lambda e, ssl=ssl: e.tensor_tensor(
                        rpB[:, :], sqb[:, :], ssl, ALU.mult))
                    em.op("dve", lambda e, ksl=ksl: e.tensor_tensor(
                        ksl, rpA[:, :], rpB[:, :], ALU.add),
                        tok=f"{sfx}_krope{tc}_{mcol_g}")
            for wnd in range(4):
                half = wnd % 2
                vww = ([f"{sfx}_vmm{tc}_{wnd-2}_3"] if wnd >= 2 else
                       [f"{sfx}_kmm{tc}_{(wnd+2)*2+1}"])
                wl = wload(KW_O, half, "Wqkv", 8 + wnd, waits=vww,
                           tag=f"{sfx}_v{tc}_{wnd}")
                for tt in range(4):
                    po = (tt % 2) * 512
                    for kc in range(DC):
                        if kc == 0 and tt >= 2:
                            em.waits_only("pe", [f"{sfx}_vev{tc}_{wnd}_{tt-2}"])
                        elif kc == 0 and wnd >= 1:
                            em.waits_only("pe", [f"{sfx}_vev{tc}_{wnd-1}_{tt+2}"])
                        elif kc == 0 and tc >= 1:
                            em.waits_only("pe", [f"{sfx}_otB{tc-1}_7",
                                                 f"{sfx}_bc_kv{tc}"])
                        elif kc == 0:
                            em.waits_only("pe", [f"{sfx}_bc_kv{tc}"])
                        em.mm(PC[:, po:po + 256],
                              A(OTH_O + kc * QC + tt * 128, 128),
                              wtile(KW_O, half, "Wqkv", kc, 0, 256),
                              start=(kc == 0), stop=(kc == DC - 1),
                              waits=[wl] if kc == 0 else (),
                              tok=f"{sfx}_vmm{tc}_{wnd}_{tt}" if kc == DC - 1
                                  else None)
                    kt = tc * 4 + tt
                    hbase = wnd * 4
                    vdst = ARENA[:, VT_O + kt * (H * 65) + hbase * 65:][:, 0:4 * 65]
                    vdst3 = vdst.rearrange("p (h w) -> p h w", w=65)[:, :, 0:64]
                    em.op("dve", lambda e, vdst3=vdst3, po=po: e.tensor_copy(
                        vdst3, PC[:, po:po + 256].rearrange("p (h w) -> p h w",
                                                            w=64)),
                        waits=[f"{sfx}_vmm{tc}_{wnd}_{tt}"],
                        tok=f"{sfx}_vev{tc}_{wnd}_{tt}")
            for c in range(DC):
                hA, hB = 2 * c, 2 * c + 1
                em.op("act", lambda e, c=c, tc=tc: e.activation(
                    knb[:, :], KRs(c, tc * QC, QC), AF.Square),
                    waits=[f"{sfx}_krope{tc}_{c}"] +
                          ([f"{sfx}_knB{c-1}_{tc}"] if c else
                           ([f"{sfx}_knB{DC-1}_{tc-1}"] if tc else [])),
                    tok=f"{sfx}_knsq{c}_{tc}")
                knw = [f"{sfx}_knsq{c}_{tc}"]
                if c:
                    knw.append(f"{sfx}_knd{c-1}_{tc}")
                elif tc:
                    knw.append(f"{sfx}_knd{DC-1}_{tc-1}")
                    knw.append(f"{sfx}_otB{tc-1}_7")
                em.mm(PD[0:1, 0:QC], onescb[0:64, 0:1], knb[0:64, :],
                      waits=knw, tok=f"{sfx}_knA{c}_{tc}")
                em.mm(PEb[0:1, 0:QC], onescb[64:128, 0:1], knb[64:128, :],
                      tok=f"{sfx}_knB{c}_{tc}")
                em.op("dve", lambda e: e.reduce_max(stat[0:1, 6:7],
                                                    PD[0:1, 0:QC], AX.X),
                      waits=[f"{sfx}_knA{c}_{tc}"])
                em.op("dve", lambda e, hA=hA, tc=tc: (
                    e.tensor_max(rowf[0:1, 1552 + hA:1553 + hA],
                                 rowf[0:1, 1552 + hA:1553 + hA], stat[0:1, 6:7])
                    if tc else
                    e.tensor_copy(rowf[0:1, 1552 + hA:1553 + hA],
                                  stat[0:1, 6:7])))
                em.op("dve", lambda e: e.reduce_max(stat[0:1, 6:7],
                                                    PEb[0:1, 0:QC], AX.X),
                      waits=[f"{sfx}_knB{c}_{tc}"])
                em.op("dve", lambda e, hB=hB, tc=tc: (
                    e.tensor_max(rowf[0:1, 1552 + hB:1553 + hB],
                                 rowf[0:1, 1552 + hB:1553 + hB], stat[0:1, 6:7])
                    if tc else
                    e.tensor_copy(rowf[0:1, 1552 + hB:1553 + hB],
                                  stat[0:1, 6:7])),
                    tok=f"{sfx}_knd{c}_{tc}")
            if step == NQC - 1:
                dbg_store(f"{sfx}_kr", A(KR_O, DC * T),
                          [f"{sfx}_krope{NQC-1}_{DC-1}"])
                dbg_store(f"{sfx}_vt", A(VT_O, NKT * H * 65),
                          [f"{sfx}_vev{NQC-1}_3_3", f"{sfx}_vones"])

            if "attn" not in phases:
                continue
            dbg_fence()
            pcg = [f"{sfx}_vev{qc}_3_3"]
            h1_chunk(QQ_O, qc, pcg, f"q{qc}")
            for mcol_g in range(8):
                half = mcol_g % 2
                if mcol_g >= 2:
                    wqw = [f"{sfx}_qmm{qc}_{mcol_g-2}"]
                elif qc:
                    wqw = [f"{sfx}_qmm{qc-1}_{mcol_g+6}"]
                else:
                    wqw = list(prev_barrier)
                wq_dst = A(WQ_O + half * 1024, 1024)
                wq_src = scr["Wqkv"][mcol_g // 2].rearrange(
                    "p (kc m) -> p kc m", m=256)[
                    :, :, (mcol_g % 2) * 128:(mcol_g % 2) * 128 + 128]
                em.dma("sync", wq_dst.rearrange("p (kc m) -> p kc m", m=128),
                       wq_src, waits=wqw + [pc_tok["Wqkv"]],
                       tok=f"{sfx}_wqld{qc}_{mcol_g}", sem=f"ws_wq{half}")
                wl = f"{sfx}_wqld{qc}_{mcol_g}"
                po = (mcol_g % 2) * 512
                for kc in range(DC):
                    if kc == 0 and mcol_g >= 2:
                        em.waits_only("pe", [f"{sfx}_qev{qc}_{mcol_g-2}"])
                    em.mm(PA[:, po:po + QC],
                          ARENA[:, WQ_O + half * 1024 + kc * 128:
                                WQ_O + half * 1024 + kc * 128 + 128],
                          A(QQ_O + kc * QC, QC),
                          start=(kc == 0), stop=(kc == DC - 1),
                          waits=([wl, f"{sfx}_h1_q{qc}"] +
                                 ([f"{sfx}_attf_{qc-1}"] if qc else []))
                                if kc == 0 else (),
                          tok=f"{sfx}_qmm{qc}_{mcol_g}" if kc == DC - 1
                              else None)
                em.op("dve", lambda e, mcol_g=mcol_g, po=po: e.tensor_copy(
                    A(OTH_O + mcol_g * QC, QC), PA[:, po:po + QC]),
                    waits=[f"{sfx}_qmm{qc}_{mcol_g}"],
                    tok=f"{sfx}_qev{qc}_{mcol_g}")
                csl = COS[:, qc * QC:(qc + 1) * QC]
                ssl = SINPM[:, qc * QC:(qc + 1) * QC]
                src = A(OTH_O + mcol_g * QC, QC)
                em.op("dve", lambda e, src=src: e.stream_shuffle(sqb[:, :], src,
                                                                 SHUF),
                      waits=[f"{sfx}_qev{qc}_{mcol_g}"])
                em.op("dve", lambda e, src=src, csl=csl: e.tensor_tensor(
                    rpA[:, :], src, csl, ALU.mult))
                em.op("dve", lambda e, ssl=ssl: e.tensor_tensor(
                    rpB[:, :], sqb[:, :], ssl, ALU.mult))
                em.op("dve", lambda e, src=src: e.tensor_tensor(
                    src, rpA[:, :], rpB[:, :], ALU.add),
                    tok=f"{sfx}_qrope{qc}_{mcol_g}")
            for c in range(DC):
                hA, hB = 2 * c, 2 * c + 1
                em.op("act", lambda e, c=c: e.activation(
                    knb[:, :], A(OTH_O + c * QC, QC), AF.Square),
                    waits=[f"{sfx}_qrope{qc}_{c}"] +
                          ([f"{sfx}_qnB{qc}_{c-1}"] if c else
                           [f"{sfx}_knB{DC-1}_{tc}"]),
                    tok=f"{sfx}_qnsq{qc}_{c}")
                qnw = [f"{sfx}_qnsq{qc}_{c}"]
                if c:
                    qnw.append(f"{sfx}_qnd{qc}_{c-1}")
                else:
                    qnw.append(f"{sfx}_knd{DC-1}_{tc}")
                em.mm(PD[0:1, 0:QC], onescb[0:64, 0:1], knb[0:64, :],
                      waits=qnw, tok=f"{sfx}_qnA{qc}_{c}")
                em.mm(PEb[0:1, 0:QC], onescb[64:128, 0:1], knb[64:128, :],
                      tok=f"{sfx}_qnB{qc}_{c}")
                em.op("dve", lambda e: e.reduce_max(stat[0:1, 6:7],
                                                    PD[0:1, 0:QC], AX.X),
                      waits=[f"{sfx}_qnA{qc}_{c}"])
                em.op("dve", lambda e, hA=hA: e.tensor_copy(
                    rowf[0:1, 1536 + hA:1537 + hA], stat[0:1, 6:7]))
                em.op("dve", lambda e: e.reduce_max(stat[0:1, 6:7],
                                                    PEb[0:1, 0:QC], AX.X),
                      waits=[f"{sfx}_qnB{qc}_{c}"])
                em.op("dve", lambda e, hB=hB: e.tensor_copy(
                    rowf[0:1, 1536 + hB:1537 + hB], stat[0:1, 6:7]),
                    tok=f"{sfx}_qnd{qc}_{c}")
            em.op("dve", lambda e: e.tensor_tensor(rowf[0:1, 1568:1584],
                                                   rowf[0:1, 1536:1552],
                                                   rowf[0:1, 1552:1568],
                                                   ALU.mult),
                  waits=[f"{sfx}_qnd{qc}_{DC-1}"] +
                        ([f"{sfx}_ex{qc-1}_7_{4*qc-1}"] if qc else []),
                  tok=f"{sfx}_cprod{qc}")
            em.op("act", lambda e: e.activation(rowf[0:1, 1568:1584],
                                                rowf[0:1, 1568:1584], AF.Ln,
                                                bias=consts[0:1, 0:1]),
                  waits=[f"{sfx}_cprod{qc}"])
            em.op("act", lambda e: e.activation(rowf[0:1, 1568:1584],
                                                rowf[0:1, 1568:1584], AF.Exp,
                                                scale=0.5),
                  tok=f"{sfx}_crow{qc}")
            em.op("dve", lambda e: e.tensor_scalar(rowb[0:1, 2 * T:2 * T + 16],
                                                   rowf[0:1, 1568:1584],
                                                   -0.125, None, ALU.mult),
                  waits=[f"{sfx}_crow{qc}"], tok=f"{sfx}_crowb{qc}")
            em.mm(PD[:, 0:16], ones1b[0:1, :], rowb[0:1, 2 * T:2 * T + 16],
                  waits=[f"{sfx}_crowb{qc}"], tok=f"{sfx}_cbm{qc}")
            em.op("dve", lambda e: e.tensor_copy(cnegc[:, :], PD[:, 0:16]),
                  waits=[f"{sfx}_cbm{qc}"])
            cv = cnegc[:, :].rearrange("p (a b) -> p a b", b=2)
            em.op("dve", lambda e, cv=cv: e.tensor_tensor(
                cnegP[:, :], cv[:, :, 0], cv[:, :, 1], ALU.min),
                tok=f"{sfx}_cnegP{qc}")
            if qc == 0:
                dbg_store(f"{sfx}_cneg", cnegc[:, :], [f"{sfx}_cnegP{qc}"])
                dbg_store(f"{sfx}_q0", A(OTH_O, DC * QC),
                          [f"{sfx}_qrope0_{DC-1}"])
            NKTq = 4 * qc + 4
            ps4v = PS4.rearrange("p (g s x) -> p g s x", g=2, s=2, x=512)
            for pr in range(8):
                hA, hB = 2 * pr, 2 * pr + 1
                qA = ARENA[0:64, OTH_O + pr * QC: OTH_O + (pr + 1) * QC]
                qB = ARENA[64:128, OTH_O + pr * QC: OTH_O + (pr + 1) * QC]

                def emit_av(kt, pr=pr, hA=hA, hB=hB):
                    p = kt % 2
                    diag = (kt >= 4 * qc)
                    vA = ARENA[:, VT_O + kt * (H * 65) + hA * 65:
                               VT_O + kt * (H * 65) + hA * 65 + 65]
                    vB = ARENA[:, VT_O + kt * (H * 65) + hB * 65:
                               VT_O + kt * (H * 65) + hB * 65 + 65]
                    wl = ([f"{sfx}_mk{qc}_{pr}_{kt}"] if diag else
                          [f"{sfx}_ex{qc}_{pr}_{kt}"])
                    if kt == 0:
                        wl.append(f"{sfx}_vones")
                        if pr >= 1:
                            wl.append(f"{sfx}_otB{qc}_{pr-1}")
                        else:
                            wl.append(f"{sfx}_qnd{qc}_{DC-1}")
                            if qc >= 1:
                                wl.append(f"{sfx}_otB{qc-1}_7")
                    em.mm(PD[0:65, :], vA, A(EA_O + p * 1024, 512),
                          start=(kt == 0), stop=(kt == NKTq - 1), waits=wl)
                    em.mm(PEb[0:65, :], vB, A(EA_O + p * 1024 + 512, 512),
                          start=(kt == 0), stop=(kt == NKTq - 1),
                          tok=f"{sfx}_av{qc}_{pr}_{kt}")

                for b in range(NKTq // 2):
                    for kt in (2 * b, 2 * b + 1):
                        p = kt % 2
                        kA = ARENA[0:64, KR_O + pr * T + kt * 128:
                                   KR_O + pr * T + (kt + 1) * 128]
                        kB = ARENA[64:128, KR_O + pr * T + kt * 128:
                                   KR_O + pr * T + (kt + 1) * 128]
                        wl = []
                        if kt == 0:
                            wl.append(f"{sfx}_qrope{qc}_{pr}")
                        if kt >= 2:
                            wl.append(f"{sfx}_ex{qc}_{pr}_{kt-2}")
                        elif pr >= 1:
                            wl.append(f"{sfx}_ex{qc}_{pr-1}_{4*qc+3}")
                        elif qc >= 1:
                            wl.append(f"{sfx}_ex{qc-1}_7_{4*qc-1}")
                        em.mm(PS4[:, p * 512: p * 512 + QC], kA, qA, waits=wl)
                        em.mm(PS4[:, 1024 + p * 512: 1024 + p * 512 + QC],
                              kB, qB, tok=f"{sfx}_s{qc}_{pr}_{kt}")
                    if b >= 1:
                        emit_av(2 * b - 2)
                        emit_av(2 * b - 1)
                    for kt in (2 * b, 2 * b + 1):
                        p = kt % 2
                        ewl = [f"{sfx}_s{qc}_{pr}_{kt}", f"{sfx}_cnegP{qc}"]
                        if kt >= 2:
                            ewl.append(f"{sfx}_av{qc}_{pr}_{kt-2}")
                        elif pr >= 1:
                            ewl.append(f"{sfx}_av{qc}_{pr-1}_{4*qc+3}")
                        elif qc >= 1:
                            ewl.append(f"{sfx}_av{qc-1}_7_{4*qc-1}")
                        em.op("act", lambda e, p=p, pr=pr: e.activation(
                            A(EA_O + p * 1024, 1024).rearrange(
                                "q (g x) -> q g x", g=2),
                            ps4v[:, :, p, :], AF.Exp,
                            bias=cnegP[:, pr:pr + 1], scale=0.125),
                            waits=ewl, tok=f"{sfx}_ex{qc}_{pr}_{kt}")
                        if kt >= 4 * qc:
                            r = kt * 128 - qc * 512
                            em.op("dve", lambda e, p=p, r=r: e.tensor_tensor(
                                A(EA_O + p * 1024, 512),
                                A(EA_O + p * 1024, 512),
                                maskB[:, 384 - r:896 - r], ALU.mult),
                                waits=[f"{sfx}_ex{qc}_{pr}_{kt}"])
                            em.op("dve", lambda e, p=p, r=r: e.tensor_tensor(
                                A(EA_O + p * 1024 + 512, 512),
                                A(EA_O + p * 1024 + 512, 512),
                                maskB[:, 384 - r:896 - r], ALU.mult),
                                tok=f"{sfx}_mk{qc}_{pr}_{kt}")
                emit_av(NKTq - 2)
                emit_av(NKTq - 1)
                em.op("dve", lambda e: e.tensor_copy(lrow[0:1, 0:512],
                                                     PD[64:65, :]),
                      waits=[f"{sfx}_av{qc}_{pr}_{NKTq-1}"])
                em.op("dve", lambda e: e.tensor_copy(lrow[0:1, 512:1024],
                                                     PEb[64:65, :]),
                      tok=f"{sfx}_lr{qc}_{pr}")
                em.mm(PC[0:64, 0:512], ones1b[0:1, 0:64], lrow[0:1, 0:512],
                      waits=[f"{sfx}_lr{qc}_{pr}"])
                em.mm(PC[64:128, 0:512], ones1b[0:1, 0:64], lrow[0:1, 512:1024],
                      tok=f"{sfx}_lb{qc}_{pr}")
                em.op("dve", lambda e: e.reciprocal(sgf[:, :], PC[:, 0:512]),
                      waits=[f"{sfx}_lb{qc}_{pr}"], tok=f"{sfx}_rcp{qc}_{pr}")
                cbase = pr * QC
                em.op("dve", lambda e, cbase=cbase: e.tensor_tensor(
                    ARENA[0:64, OTH_O + cbase: OTH_O + cbase + QC],
                    PD[0:64, :], sgf[0:64, :], ALU.mult),
                    waits=[f"{sfx}_rcp{qc}_{pr}"],
                    tok=f"{sfx}_otA{qc}_{pr}")
                em.op("dve", lambda e, cbase=cbase: e.tensor_tensor(
                    ARENA[64:128, OTH_O + cbase: OTH_O + cbase + QC],
                    PEb[0:64, :], sgf[64:128, :], ALU.mult),
                    tok=f"{sfx}_otB{qc}_{pr}")
            if qc == 0:
                dbg_store(f"{sfx}_ot0", A(OTH_O, DC * QC), [f"{sfx}_otB{qc}_7"])

            for wnd in range(8):
                half = wnd % 2
                if wnd >= 2:
                    wow = [f"{sfx}_womm{qc}_{wnd-2}"]
                elif qc:
                    wow = [f"{sfx}_womm{qc-1}_{wnd+6}"]
                else:
                    wow = list(prev_barrier)
                wl = wload(WO_O, half, "Wo", wnd, stream="wo", waits=wow,
                           tag=f"{sfx}_wo{qc}_{wnd}")
                po = (wnd % 2) * 512
                for kc in range(DC):
                    if kc == 0 and wnd >= 2:
                        em.waits_only("pe", [f"{sfx}_woR{qc}_{wnd-2}"])
                    em.mm(PC[:, po:po + QC],
                          wtile(WO_O, half, "Wo", kc, 0, 128),
                          A(OTH_O + kc * QC, QC),
                          start=(kc == 0), stop=(kc == DC - 1),
                          waits=[wl, f"{sfx}_otB{qc}_7", f"{sfx}_otA{qc}_7"]
                                if kc == 0 else (),
                          tok=f"{sfx}_womm{qc}_{wnd}" if kc == DC - 1 else None)
                em.op("dve", lambda e, wnd=wnd, po=po: e.tensor_tensor(
                    HHs(wnd, qc * QC, QC), HHs(wnd, qc * QC, QC),
                    PC[:, po:po + QC], ALU.add),
                    waits=[f"{sfx}_womm{qc}_{wnd}"],
                    tok=f"{sfx}_wox{qc}_{wnd}")
                em.op("dve", lambda e, wnd=wnd, po=po: e.tensor_tensor(
                    Rs(wnd, qc * QC, QC), Rs(wnd, qc * QC, QC),
                    PC[:, po:po + QC], ALU.add),
                    tok=f"{sfx}_woR{qc}_{wnd}")
            em.op("dve", lambda e: e.memset(stat[:, 7:8], 0.0),
                  waits=[f"{sfx}_woR{qc}_7"], tok=f"{sfx}_wo_pf_{qc}")
            em.op("dve", lambda e: e.memset(stat[:, 7:8], 0.0),
                  waits=[f"{sfx}_woR{qc}_7", f"{sfx}_otB{qc}_7"],
                  tok=f"{sfx}_attf_{qc}")
        dbg_store(f"{sfx}_x1", A(HH_O, DC * T), [f"{sfx}_wo_pf_{NQC-1}"])

        def ln_rows(xoff, tt, guard_mm, guard_sq, tag):
            for c in range(DC):
                xsl = A(xoff + c * T + tt * QC, QC)
                em.mm(PD[0:1, 0:QC], onescb[:, 0:1], xsl,
                      start=(c == 0), stop=(c == DC - 1),
                      waits=guard_mm if c == 0 else (),
                      tok=f"{tag}_sum" if c == DC - 1 else None)
            for c in range(DC):
                xsl = A(xoff + c * T + tt * QC, QC)
                em.op("act", lambda e, xsl=xsl: e.activation(sqb[:, :], xsl,
                                                             AF.Square),
                      waits=guard_sq + ([f"{tag}_ss{c-1}"] if c else []),
                      tok=f"{tag}_sq{c}")
                em.mm(PEb[0:1, 0:QC], onescb[:, 0:1], sqb[:, :],
                      start=(c == 0), stop=(c == DC - 1),
                      waits=[f"{tag}_sq{c}"],
                      tok=f"{tag}_ss{c}")
            em.op("dve", lambda e: e.tensor_scalar(rowf[0:1, 512:512 + QC],
                                                   PD[0:1, 0:QC], 1.0 / D, None,
                                                   ALU.mult),
                  waits=[f"{tag}_sum"])
            em.op("dve", lambda e: e.tensor_mul(rowf[0:1, 1024:1024 + QC],
                                                rowf[0:1, 512:512 + QC],
                                                rowf[0:1, 512:512 + QC]))
            em.op("dve", lambda e: e.scalar_tensor_tensor(
                rowf[0:1, 1024:1024 + QC], PEb[0:1, 0:QC], 1.0 / D,
                rowf[0:1, 1024:1024 + QC], ALU.mult, ALU.subtract),
                waits=[f"{tag}_ss{DC-1}"], tok=f"{tag}_var")
            em.op("act", lambda e: e.activation(rowf[0:1, 1024:1024 + QC],
                                                rowf[0:1, 1024:1024 + QC], AF.Ln,
                                                bias=consts[0:1, 0:1]),
                  waits=[f"{tag}_var"])
            em.op("act", lambda e: e.activation(rowf[0:1, 1024:1024 + QC],
                                                rowf[0:1, 1024:1024 + QC],
                                                AF.Exp, scale=-0.5),
                  tok=f"{tag}_r")
            em.op("dve", lambda e: e.tensor_copy(rowb[0:1, 2 * T + 64:
                                                      2 * T + 64 + QC],
                                                 rowf[0:1, 512:512 + QC]),
                  waits=[f"{tag}_r"])
            em.op("dve", lambda e: e.tensor_copy(rowb[0:1, 2 * T + 640:
                                                      2 * T + 640 + QC],
                                                 rowf[0:1, 1024:1024 + QC]),
                  tok=f"{tag}_rows")
            em.mm(PC[:, 0:QC], ones1b[0:1, :],
                  rowb[0:1, 2 * T + 64:2 * T + 64 + QC],
                  waits=[f"{tag}_rows"], tok=f"{tag}_bmu")
            em.mm(PC[:, 512:512 + QC], ones1b[0:1, :],
                  rowb[0:1, 2 * T + 640:2 * T + 640 + QC], tok=f"{tag}_br")
            em.op("dve", lambda e: e.tensor_copy(bc1[:, :], PC[:, 0:QC]),
                  waits=[f"{tag}_bmu"])
            em.op("dve", lambda e: e.tensor_copy(bc2[:, :], PC[:, 512:512 + QC]),
                  waits=[f"{tag}_br"], tok=f"{tag}_bc")

        if "ffn" not in phases:
            continue
        for tt in range(NQC):
            dbg_fence()
            tag = f"{sfx}_f{tt}"
            gmm = [f"{sfx}_wo_pf_{NQC-1}"]
            gsq = [f"{sfx}_attf_{NQC-1}"]
            if tt:
                gmm.append(f"{sfx}_f{tt-1}_var")
                gsq.append(f"{sfx}_f{tt-1}_ss{DC-1}")
            ln_rows(HH_O, tt, gmm, gsq, tag)
            for c in range(DC):
                dst = A(H1F_O + c * QC, QC)
                em.op("dve", lambda e, dst=dst, c=c: e.tensor_tensor(
                    dst, HHs(c, tt * QC, QC), bc1[:, :], ALU.subtract),
                    waits=[f"{tag}_bc"])
                em.op("dve", lambda e, dst=dst: e.tensor_tensor(
                    dst, dst, bc2[:, :], ALU.mult))
                em.op("dve", lambda e, dst=dst, c=c: e.tensor_scalar(
                    dst, dst, ln2g[:, c:c + 1], ln2b_[:, c:c + 1],
                    ALU.mult, ALU.add),
                    tok=f"{tag}_h2" if c == DC - 1 else None)
            for wnd in range(8):
                half = wnd % 2
                wl = wload(W1_O, half, "W1", wnd, stream="w1",
                           waits=[f"{tag}_gel{wnd-2}_3"] if wnd >= 2 else
                                 ([f"{sfx}_f{tt-1}_w2f"] if tt else
                                  [f"{sfx}_attf_{NQC-1}"]),
                           tag=f"{tag}_w1_{wnd}")
                for mc in range(4):
                    mcol_g = wnd * 4 + mc
                    po = (mc % 2) * 512
                    for kc in range(DC):
                        if kc == 0 and mc >= 2:
                            em.waits_only("pe", [f"{tag}_gel{wnd}_{mc-2}"])
                        elif kc == 0 and wnd >= 1:
                            em.waits_only("pe", [f"{tag}_gel{wnd-1}_{mc+2}"])
                        elif kc == 0 and tt >= 1:
                            em.waits_only("pe", [f"{sfx}_f{tt-1}_gel7_{mc+2}"])
                        em.mm(PA[:, po:po + QC],
                              wtile(W1_O, half, "W1", kc, mc * 128, 128),
                              A(H1F_O + kc * QC, QC),
                              start=(kc == 0), stop=(kc == DC - 1),
                              waits=[wl, f"{tag}_h2"] if kc == 0 else (),
                              tok=f"{tag}_w1mm{mcol_g}" if kc == DC - 1
                                  else None)
                    em.op("act", lambda e, mcol_g=mcol_g, po=po: e.activation(
                        A(GG_O + mcol_g * QC, QC), PA[:, po:po + QC],
                        AF.Gelu, bias=b1c[:, mcol_g:mcol_g + 1]),
                        waits=[f"{tag}_w1mm{mcol_g}"],
                        tok=f"{tag}_gel{wnd}_{mc}")
            if tt == 0:
                dbg_store(f"{sfx}_g00", A(GG_O, 32 * QC), [f"{tag}_gel7_3"])
            for wnd in range(8):
                half = wnd % 2
                wl = wload(W2_O, half, "W2", wnd, stream="w2",
                           waits=[f"{tag}_xa{wnd-2}"] if wnd >= 2 else
                                 ([f"{sfx}_f{tt-1}_w2f"] if tt else
                                  [f"{sfx}_attf_{NQC-1}"]),
                           tag=f"{tag}_w2_{wnd}")
                po = (wnd % 2) * 512
                for kc in range(32):
                    em.mm(PB[:, po:po + QC],
                          wtile(W2_O, half, "W2", kc, 0, 128),
                          A(GG_O + kc * QC, QC),
                          start=(kc == 0), stop=(kc == 31),
                          waits=[wl, f"{tag}_gel7_3"] if kc == 0 else (),
                          tok=f"{tag}_w2mm{wnd}" if kc == 31 else None)
                em.op("dve", lambda e, wnd=wnd, po=po: e.scalar_tensor_tensor(
                    Rs(wnd, tt * QC, QC), PB[:, po:po + QC],
                    b2c[:, wnd:wnd + 1], Rs(wnd, tt * QC, QC),
                    ALU.add, ALU.add),
                    waits=[f"{tag}_w2mm{wnd}"], tok=f"{tag}_xa{wnd}")
            em.op("dve", lambda e: e.memset(stat[:, 7:8], 0.0),
                  waits=[f"{tag}_xa7"], tok=f"{tag}_w2f")
        dbg_store(f"{sfx}_xa", A(R_O, DC * T), [f"{sfx}_f{NQC-1}_w2f"])

        moecut = ("full" if "moe" in phases else
                  "p1" if "moe_p1" in phases else
                  "gate" if "moe_gate" in phases else
                  "top2" if "moe_top2" in phases else
                  "sct" if "moe_sct" in phases else
                  "tronly" if "moe_tronly" in phases else
                  "gmm" if "moe_gmm" in phases else
                  "hm" if "moe_hm" in phases else
                  "e16" if "moe_e16" in phases else None)
        if moecut is None:
            continue
        em.op("pool", lambda e: e.memset(A(E16_O, 2048), 1.0),
              waits=[f"{sfx}_wo_pf_{NQC-1}"], tok=f"{sfx}_E16m")
        for e_ in range(16):
            em.op("pool", lambda e, e_=e_: e.affine_select(
                ARENA[0:16, E16_O + e_ * 128:E16_O + (e_ + 1) * 128],
                ARENA[0:16, E16_O + e_ * 128:E16_O + (e_ + 1) * 128],
                [[0, 128]], ALU.is_equal, 0.0, base=-e_, channel_multiplier=1),
                waits=[f"{sfx}_E16m"],
                tok=f"{sfx}_E16" if e_ == 15 else None)
        st_toks = []
        if moecut == "e16":
            continue
        for tt in range(NQC):
            dbg_fence()
            tag = f"{sfx}_m{tt}"
            guard = ([f"{sfx}_f{NQC-1}_w2f"] if tt == 0 else
                     ([f"{sfx}_m{tt-1}_p2f"] if moecut == "full" else
                      ([f"{sfx}_m{tt-1}_z16_3"] if moecut == "p1" else
                       ([f"{sfx}_m{tt-1}_rtt"] if moecut == "gate" else
                        ([f"{sfx}_m{tt-1}_rt3"] if moecut == "top2" else
                         ([f"{sfx}_m{tt-1}_sct3"] if moecut == "sct" else
                          ([f"{sfx}_m{tt-1}_str3"] if moecut == "tronly" else
                           ([f"{sfx}_m{tt-1}_sig"] if moecut == "gmm" else
                           [f"{sfx}_m{tt-1}_hm"]))))))))
            gsq = guard + [f"{sfx}_f{NQC-1}_ss{DC-1}" if tt == 0 else
                           f"{sfx}_m{tt-1}_ss{DC-1}"]
            ln_rows(R_O, tt, guard, gsq, tag)
            for c in range(DC):
                dst = A(HM_O + c * QC, QC)
                em.op("dve", lambda e, dst=dst, c=c: e.tensor_tensor(
                    dst, Rs(c, tt * QC, QC), bc1[:, :], ALU.subtract),
                    waits=[f"{tag}_bc"])
                em.op("dve", lambda e, dst=dst: e.tensor_tensor(
                    dst, dst, bc2[:, :], ALU.mult),
                    tok=f"{tag}_hm" if c == DC - 1 else None)
            if tt == 0:
                dbg_store(f"{sfx}_hm0", A(HM_O, DC * QC), [f"{tag}_hm"])
            if moecut == "hm":
                continue
            for kc in range(DC):
                em.mm(PD[0:16, 0:QC], gwT[:, kc * 16:(kc + 1) * 16],
                      A(HM_O + kc * QC, QC),
                      start=(kc == 0), stop=(kc == DC - 1),
                      waits=[f"{tag}_hm", "gwT"] if kc == 0 else (),
                      tok=f"{tag}_gmm" if kc == DC - 1 else None)
            em.op("act", lambda e: e.activation(SCT[0:16, :], PD[0:16, 0:QC],
                                                AF.Sigmoid),
                  waits=[f"{tag}_gmm"], tok=f"{tag}_sig")
            if tt == 0:
                dbg_store(f"{sfx}_scores0", SCT[0:16, :], [f"{tag}_sig"])
            if moecut == "gmm":
                continue
            for i4 in range(4):
                em.tr(PC[:, 256 + i4 * 16:256 + (i4 + 1) * 16],
                      SCT[0:16, i4 * 128:(i4 + 1) * 128], identf[0:16, 0:16],
                      waits=[f"{tag}_sig"], tok=f"{tag}_str{i4}")
            if moecut == "tronly":
                continue
            for i4 in range(4):
                em.op("dve", lambda e, i4=i4: e.tensor_copy(
                    SCt[:, i4 * 16:(i4 + 1) * 16],
                    PC[:, 256 + i4 * 16:256 + (i4 + 1) * 16]),
                    waits=[f"{tag}_str3"],
                    tok=f"{tag}_sct{i4}")
            if moecut in ("sct", "tronly"):
                continue
            for i4 in range(4):
                sc = SCt[:, i4 * 16:(i4 + 1) * 16]
                em.op("dve", lambda e, sc=sc: e.reduce_max(stat[:, 0:1], sc, AX.X),
                      waits=[f"{tag}_sct{i4}"])
                em.op("dve", lambda e, sc=sc: e.tensor_scalar(
                    sgf[:, 0:16], sc, stat[:, 0:1], None, ALU.is_ge))
                em.op("dve", lambda e: e.reduce_sum(stat[:, 1:2], sgf[:, 0:16],
                                                    AX.X))
                em.op("dve", lambda e, sc=sc: e.scalar_tensor_tensor(
                    z1f[:, 0:16], sgf[:, 0:16], -1e30, sc, ALU.mult, ALU.add))
                em.op("dve", lambda e: e.reduce_max(stat[:, 2:3], z1f[:, 0:16],
                                                    AX.X))
                em.op("dve", lambda e: e.tensor_scalar(stat[:, 3:4], stat[:, 1:2],
                                                       2.0, None, ALU.is_ge))
                em.op("dve", lambda e: e.tensor_sub(stat[:, 4:5], stat[:, 0:1],
                                                    stat[:, 2:3]))
                em.op("dve", lambda e: e.scalar_tensor_tensor(
                    stat[:, 2:3], stat[:, 3:4], stat[:, 4:5], stat[:, 2:3],
                    ALU.mult, ALU.add))
                em.op("dve", lambda e: e.tensor_add(stat[:, 5:6], stat[:, 0:1],
                                                    stat[:, 2:3]))
                em.op("dve", lambda e: e.reciprocal(stat[:, 6:7], stat[:, 5:6]))
                em.op("dve", lambda e: e.tensor_scalar(stat[:, 6:7], stat[:, 6:7],
                                                       2.5, None, ALU.mult))
                em.op("dve", lambda e, sc=sc: e.tensor_scalar(
                    sgf[:, 0:16], sc, stat[:, 2:3], None, ALU.is_ge))
                em.op("dve", lambda e, sc=sc: e.tensor_mul(sgf[:, 0:16],
                                                           sgf[:, 0:16], sc))
                em.op("dve", lambda e: e.tensor_scalar(sgf[:, 0:16], sgf[:, 0:16],
                                                       stat[:, 6:7], None,
                                                       ALU.mult),
                      tok=f"{tag}_rt{i4}")
                if moecut == "top2":
                    continue
                em.tr(PD[0:16, i4 * 128:(i4 + 1) * 128],
                      sgf[:, 0:16], identf[:, :],
                      waits=[f"{tag}_rt{i4}"], tok=f"{tag}_rtr{i4}")
                em.op("dve", lambda e, i4=i4: e.tensor_copy(
                    RTT[0:16, i4 * 128:(i4 + 1) * 128],
                    PD[0:16, i4 * 128:(i4 + 1) * 128]),
                    waits=[f"{tag}_rtr{i4}"],
                    tok=f"{tag}_rtt" if i4 == 3 else None)
            if tt == 0:
                dbg_store(f"{sfx}_routeT0", RTT[0:16, :], [f"{tag}_rtt"])

            if moecut == "gate":
                continue
            for e_ in range(17):
                shared = (e_ == 16)
                wgn = "ws_gate" if shared else f"wg{e_}"
                wun = "ws_up" if shared else f"wu{e_}"
                half = e_ % 2
                base_w = ([f"{tag}_z{e_-2}_3"] if e_ >= 2 else
                          ([f"{sfx}_m{tt-1}_p2f" if moecut == "full" else
                            f"{sfx}_m{tt-1}_z16_3"] if tt else
                           [f"{sfx}_f{NQC-1}_w2f"]))
                wlg = wload(WG_O, half, wgn, 0, waits=base_w,
                            tag=f"{tag}_wg{e_}", stream="wg")
                wlu = wload(WU_O, half, wun, 0, waits=base_w,
                            tag=f"{tag}_wu{e_}", stream="wu")
                if not shared:
                    rb_psum = PEb[:, 0:QC] if e_ % 2 == 0 else PC[:, 512:512 + QC]
                    em.mm(rb_psum, ARENA[0:16, E16_O + e_ * 128:
                                         E16_O + (e_ + 1) * 128],
                          RTT[0:16, :],
                          waits=[f"{tag}_rtt", f"{sfx}_E16"] +
                                ([f"{tag}_rbf{e_-2}"] if e_ >= 2 else []),
                          tok=f"{tag}_rb{e_}")
                for m_ in range(4):
                    po = (m_ % 2) * 512
                    gwl = [wlg, f"{tag}_hm"]
                    uwl = [wlu]
                    if m_ < 2 and e_ >= 1:
                        gwl.append(f"{tag}_sl{e_-1}_{m_+2}")
                        uwl.append(f"{tag}_z{e_-1}_{m_+2}")
                    for kc in range(DC):
                        if kc == 0 and m_ >= 2:
                            em.waits_only("pe", [f"{tag}_z{e_}_{m_-2}"])
                        em.mm(PA[:, po:po + QC],
                              wtile(WG_O, half, wgn, kc, m_ * 128, 128),
                              A(HM_O + kc * QC, QC),
                              start=(kc == 0), stop=(kc == DC - 1),
                              waits=gwl if kc == 0 else (),
                              tok=f"{tag}_gp{e_}_{m_}" if kc == DC - 1 else None)
                    for kc in range(DC):
                        em.mm(PB[:, po:po + QC],
                              wtile(WU_O, half, wun, kc, m_ * 128, 128),
                              A(HM_O + kc * QC, QC),
                              start=(kc == 0), stop=(kc == DC - 1),
                              waits=uwl if kc == 0 else (),
                              tok=f"{tag}_up{e_}_{m_}" if kc == DC - 1 else None)
                    em.op("act", lambda e, po=po: e.activation(
                        sgf[:, :], PA[:, po:po + QC], AF.Silu),
                        waits=[f"{tag}_gp{e_}_{m_}"] +
                              ([f"{tag}_z{e_}_{m_-1}"] if m_ else
                               ([f"{tag}_z{e_-1}_3"] if e_ else
                                [f"{tag}_rtt"])),
                        tok=f"{tag}_sl{e_}_{m_}")
                    zdst = A(ZZ_O + (e_ * 4 + m_) * QC, QC)
                    if shared:
                        em.op("dve", lambda e, zdst=zdst, po=po: e.tensor_tensor(
                            zdst, sgf[:, :], PB[:, po:po + QC], ALU.mult),
                            waits=[f"{tag}_sl{e_}_{m_}", f"{tag}_up{e_}_{m_}"],
                            tok=f"{tag}_z{e_}_{m_}")
                    else:
                        em.op("dve", lambda e, po=po: e.tensor_tensor(
                            z1f[:, :], sgf[:, :], PB[:, po:po + QC], ALU.mult),
                            waits=[f"{tag}_sl{e_}_{m_}", f"{tag}_up{e_}_{m_}"])
                        rb_psum = (PEb[:, 0:QC] if e_ % 2 == 0
                                   else PC[:, 512:512 + QC])
                        em.op("dve", lambda e, zdst=zdst, rb_psum=rb_psum:
                              e.tensor_tensor(zdst, z1f[:, :], rb_psum,
                                              ALU.mult),
                              waits=[f"{tag}_rb{e_}"],
                              tok=f"{tag}_z{e_}_{m_}")
                if not shared:
                    em.op("dve", lambda e: e.tensor_copy(stat[:, 7:8],
                                                         stat[:, 7:8]),
                          waits=[f"{tag}_z{e_}_3"], tok=f"{tag}_rbf{e_}")
            if tt == 0:
                dbg_store(f"{sfx}_z00", A(ZZ_O, 4 * QC), [f"{tag}_z0_3"])

            if moecut == "p1":
                continue
            banks = [PA[:, 0:512], PA[:, 512:1024], PB[:, 0:512],
                     PB[:, 512:1024], PC[:, 0:512], PC[:, 512:1024],
                     PD[:, 0:512], PEb[:, 0:512]]
            for e_ in range(17):
                wdn = "ws_down" if e_ == 16 else f"wd{e_}"
                wl = wload(WDM_O, e_ % 2, wdn, 0, stream="wd",
                           waits=[f"{tag}_d{e_-2}"] if e_ >= 2 else
                                 [f"{tag}_z16_3"],
                           tag=f"{tag}_wd{e_}")
                for mc in range(DC):
                    for kc in range(4):
                        em.mm(banks[mc],
                              wtile(WDM_O, e_ % 2, wdn, kc, mc * 128, 128),
                              A(ZZ_O + (e_ * 4 + kc) * QC, QC),
                              start=(e_ == 0 and kc == 0),
                              stop=(e_ == 16 and kc == 3),
                              waits=[wl, f"{tag}_z{e_}_3"] if (kc == 0 and
                                                              mc == 0) else (),
                              tok=f"{tag}_d{e_}" if (mc == DC - 1 and kc == 3)
                                  else None)
            dbg_fence()
            OUTF = Af(OF_O, 4096)
            OUTT = Af(OU_O, 2048)
            for mc in range(DC):
                em.op("dve", lambda e, mc=mc: e.tensor_tensor(
                    OUTF[:, mc * 512:(mc + 1) * 512], Rs(mc, tt * QC, QC),
                    banks[mc], ALU.add),
                    waits=[f"{tag}_d16"], tok=f"{tag}_oev{mc}")
            for i4 in range(4):
                for mc in range(DC):
                    em.tr(PA[:, mc * 128:(mc + 1) * 128],
                          OUTF[:, mc * 512 + i4 * 128: mc * 512 + i4 * 128 + 128],
                          identf[:, :],
                          waits=([f"{tag}_oev{mc}"] if i4 == 0 else
                                 [f"{tag}_oc{i4-1}_{DC-1}"]),
                          tok=f"{tag}_otr{i4}_{mc}")
                for mc in range(DC):
                    ocw = [f"{tag}_otr{i4}_{DC-1}"]
                    if i4 >= 2 and mc == 0:
                        ocw.append(f"{tag}_st{i4-2}")
                    em.op("dve", lambda e, i4=i4, mc=mc: e.tensor_copy(
                        OUTT[:, (i4 % 2) * D + mc * 128:
                             (i4 % 2) * D + (mc + 1) * 128],
                        PA[:, mc * 128:(mc + 1) * 128]),
                        waits=ocw,
                        tok=f"{tag}_oc{i4}_{mc}")
                em.dma("act",
                       out_h[s * T + tt * QC + i4 * 128:
                             s * T + tt * QC + (i4 + 1) * 128, :],
                       OUTT[:, (i4 % 2) * D:(i4 % 2) * D + D],
                       waits=[f"{tag}_oc{i4}_{mc_}" for mc_ in range(DC)],
                       tok=f"{tag}_st{i4}", sem=f"st{i4 % 2}")
                st_toks.append(f"{tag}_st{i4}")
            em.op("dve", lambda e: e.memset(stat[:, 7:8], 0.0),
                  waits=[f"{tag}_st2", f"{tag}_st3", f"{tag}_oc3_7"],
                  tok=f"{tag}_p2f")
        if moecut == "full":
            prev_barrier = [f"{sfx}_m{NQC-1}_p2f"]
        else:
            prev_barrier = [(q, em.cnt[q]) for q in ("dve", "act", "pe", "pool")
                            if em.cnt[q] > 0]

    dmasems = [s for s in em.semnames
               if s not in ("pe", "dve", "act", "pool")]
    fin = [(s, em.cnt[s]) for s in dmasems if em.cnt[s] > 0]
    fin += [("dve", em.cnt["dve"]), ("act", em.cnt["act"]),
            ("pool", em.cnt["pool"])]
    em.waits_only("sync", fin)
    em.waits_only("act", fin)




def kernel(**inputs):
    x = np.ascontiguousarray(inputs["L0_attn_x"])
    nc = build_nc()
    base = {}
    for k, v in inputs.items():
        if k == "L0_attn_x":
            continue
        name = k.replace("L0_attn_", "").replace("L0_ffn_", "")
        base[name] = np.ascontiguousarray(v)
    in_maps = []
    for c in range(NCORES):
        m = dict(base)
        m["x"] = x[c * NSEQ:(c + 1) * NSEQ].reshape(NSEQ * T, D)
        in_maps.append(m)
    res = run_bass_kernel_spmd(nc, in_maps, core_ids=list(range(NCORES)))
    return np.concatenate(
        [np.asarray(r["out"]).reshape(NSEQ, T, D) for r in res.results], axis=0
    ).reshape(B, T, D)
